# version 38
# baseline (speedup 1.0000x reference)
import contextlib
import numpy as np
import concourse.bass as bass
import concourse.mybir as mybir
from concourse.bass_utils import run_bass_kernel_spmd

F32 = mybir.dt.float32
F32R = mybir.dt.float32r
BF16 = mybir.dt.bfloat16
AF = mybir.ActivationFunctionType
OP = mybir.AluOpType
AX = mybir.AxisListType

P = 128
T = 2048
DM = 1024
DI = 2048
DS = 16
DTR = 64
NG = DI // P
GC = DM // P
NT = T // P
EPS = 1e-5
ARENA = 46848
ARENA_F32R = 4224

ENGINES = ("sync", "tensor", "vector", "scalar", "gpsimd")
SEMS = ("dS", "dA", "dG", "cV", "cS", "cT", "cG", "uAB", "uYM", "uP", "uYF", "dM", "dN", "dH")


class Sched:

    def __init__(self):
        self.items = {e: [] for e in ENGINES}
        self.cnt = {s: 0 for s in SEMS}
        self.mark = {e: {s: 0 for s in SEMS} for e in ENGINES}

    def wait(self, eng, sem, val):
        if val <= 0 or self.mark[eng][sem] >= val:
            return
        self.mark[eng][sem] = val
        self.items[eng].append(("w", sem, val))

    def waitt(self, eng, tok):
        if tok is not None:
            self.wait(eng, tok[0], tok[1])

    def emit(self, eng, fn, inc=None, amt=1):
        tok = None
        if inc is not None:
            self.cnt[inc] += amt
            tok = (inc, self.cnt[inc])
        self.items[eng].append(("e", fn, inc, amt))
        return tok

    def dma(self, q, out, in_, sem, mark=True):
        self.emit(q, lambda e, o=out, i=in_: e.dma_start(out=o, in_=i), inc=sem, amt=16)
        if mark:
            return self.marker(q, sem)
        return None

    def marker(self, q, dsem):
        msem = {"dS": "dM", "dG": "dH"}.get(dsem, "dN")
        self.wait(q, dsem, self.cnt[dsem])
        self.cnt[msem] += 1
        tok = (msem, self.cnt[msem])
        self.items[q].append(("i", msem))
        return tok


def build_nc():
    nc = bass.Bass("TRN2", target_bir_lowering=False)

    x_d = nc.dram_tensor("x", [T, DM], F32, kind="ExternalInput").ap()
    w_in_d = nc.dram_tensor("w_in", [P, GC, 2 * DI], F32, kind="ExternalInput").ap()
    w_xp_d = nc.dram_tensor("w_xp", [P, NG, 96], F32, kind="ExternalInput").ap()
    w_dt_d = nc.dram_tensor("w_dt", [DTR, DI], F32, kind="ExternalInput").ap()
    w_out_d = nc.dram_tensor("w_out", [P, NG, DM], F32, kind="ExternalInput").ap()
    w_r_d = nc.dram_tensor("w_r", [P, GC, DM], F32, kind="ExternalInput").ap()
    w_k_d = nc.dram_tensor("w_k", [P, GC, DM], F32, kind="ExternalInput").ap()
    w_v_d = nc.dram_tensor("w_v", [P, GC, DM], F32, kind="ExternalInput").ap()
    alog_d = nc.dram_tensor("a_log", [P, NG, DS], F32, kind="ExternalInput").ap()
    cw_d = nc.dram_tensor("conv_w", [P, NG, 4], F32, kind="ExternalInput").ap()
    cb_d = nc.dram_tensor("conv_b", [P, NG], F32, kind="ExternalInput").ap()
    bdt_d = nc.dram_tensor("b_dt", [P, NG], F32, kind="ExternalInput").ap()
    dpar_d = nc.dram_tensor("d_par", [P, NG], F32, kind="ExternalInput").ap()
    mur_d = nc.dram_tensor("mu_r", [P, GC], F32, kind="ExternalInput").ap()
    muk_d = nc.dram_tensor("mu_k", [P, GC], F32, kind="ExternalInput").ap()
    idr_d = nc.dram_tensor("identr", [P, P], F32R, kind="ExternalInput").ap()
    id_d = nc.dram_tensor("ident", [P, P], F32, kind="ExternalInput").ap()
    out_d = nc.dram_tensor("out", [T, DM], F32, kind="ExternalOutput").ap()

    s_xb = nc.dram_tensor("s_xb", [P, NG, T], F32, kind="Internal").ap()
    s_xc = nc.dram_tensor("s_xc", [P, NG, T], F32, kind="Internal").ap()
    s_dl = nc.dram_tensor("s_dl", [P, NG, T], F32, kind="Internal").ap()
    s_sz = nc.dram_tensor("s_sz", [P, NG, T], F32, kind="Internal").ap()
    s_xn = nc.dram_tensor("s_xn", [P, GC, T], F32, kind="Internal").ap()
    s_bc = nc.dram_tensor("s_bc", [2 * DS, T], F32, kind="Internal").ap()
    s_y = nc.dram_tensor("s_y", [P, NG, T], BF16, kind="Internal").ap()
    s_wq = nc.dram_tensor("s_wq", [P, 16, 1024], BF16, kind="Internal").ap()

    with contextlib.ExitStack() as stk:
        E = stk.enter_context
        ar = E(nc.sbuf_tensor("arena", [P, ARENA], F32))
        arF = E(nc.sbuf_tensor("arenaF", [P, ARENA_F32R], F32R))
        psA = E(nc.psum_tensor("psA", [P, 2048], F32))
        psB = E(nc.psum_tensor("psB", [P, 2048], F32))
        sems = {s: E(nc.semaphore(s)) for s in SEMS}
        block = E(nc.Block())

        S = Sched()

        def A(c0, n, rows=slice(0, P)):
            return ar[rows, c0:c0 + n]

        a_scale = A(0, 256)
        conv_w = A(256, 64)
        conv_b = A(320, 16)
        b_dt = A(336, 16)
        d_par = A(352, 16)
        mu_r = A(368, 8)
        mu_k = A(376, 8)
        id_f32 = A(384, 128)
        id_f32r = arF[:, 4096:4224]
        stt_sum = A(640, 16)
        stt_sq = A(656, 16)
        stt_m = A(672, 16)
        stt_v = A(688, 16)
        stt_rs = A(704, 16)
        stt_nb = A(720, 16)
        stt_t = A(736, 16)
        eps_ap = A(752, 1)
        neg1_ap = A(753, 1)

        S.emit("vector", lambda e: e.memset(eps_ap[:], EPS))
        tok_consts = S.emit("vector", lambda e: e.memset(neg1_ap[:], -1.0), inc="cV")
        tok_id = None
        tok_idr = None
        for dst, src in [(a_scale, alog_d), (conv_w, cw_d), (conv_b, cb_d),
                         (b_dt, bdt_d), (d_par, dpar_d), (mu_r, mur_d), (mu_k, muk_d)]:
            S.dma("sync", dst, src, "dS", mark=False)
        S.dma("sync", id_f32, id_d, "dS", mark=False)
        S.dma("sync", id_f32r, idr_d, "dS", mark=False)
        tok_par_last = tok_id = tok_idr = S.marker("sync", "dS")
        w_xp = A(43784, 768).bitcast(BF16).rearrange("p (g m) -> p g m", g=NG)
        w_dt = A(44552, 1024, slice(0, DTR)).bitcast(BF16)
        wxp_stage = A(35592, 1536)
        wdt_stage = A(37640, 2048, slice(0, DTR))
        t_ = S.dma("sync", wxp_stage, w_xp_d.rearrange("p g m -> p (g m)"), "dS")
        S.waitt("scalar", t_)
        tok_wxp = S.emit("scalar", lambda e: e.copy(
            out=A(43784, 768).bitcast(BF16), in_=wxp_stage), inc="cS")
        t_ = S.dma("sync", wdt_stage, w_dt_d, "dS")
        S.waitt("scalar", t_)
        tok_wdt = S.emit("scalar", lambda e: e.copy(
            out=A(44552, 1024, slice(0, DTR)).bitcast(BF16),
            in_=wdt_stage), inc="cS")

        x_full = A(768, NT * 1024)
        xn_bf_full = A(17152, 8192).bitcast(BF16)
        junk = A(25344, 1024)
        xn_tok = [A(26368, 1024), A(27392, 1024)]
        xn_bnc = [A(28416, 1024), A(29440, 1024)]
        ps_half = [psA[:, 0:1024], psA[:, 1024:2048]]

        tok_x = []
        for i in range(NT):
            S.dma("sync", x_full[:, i * 1024:(i + 1) * 1024],
                  x_d.rearrange("(n p) c -> n p c", p=P)[i], "dS", mark=False)
            if i % 8 == 7:
                mk = S.marker("sync", "dS")
                tok_x.extend([mk] * 8)
        last_vstat = None
        last_sstat = None
        for i in range(NT):
            S.waitt("vector", tok_x[i])
            last_vstat = S.emit("vector", lambda e, i=i: e.tensor_reduce(
                out=stt_sum[:, i:i + 1], in_=x_full[:, i * 1024:(i + 1) * 1024],
                axis=AX.X, op=OP.add), inc="cV")
            S.waitt("scalar", tok_x[i])
            S.waitt("scalar", last_sstat)
            last_sstat = S.emit("scalar", lambda e, i=i: e.activation(
                out=junk[:], in_=x_full[:, i * 1024:(i + 1) * 1024], func=AF.Square,
                accum_out=stt_sq[:, i:i + 1]), inc="cS")
        S.waitt("vector", last_sstat)
        S.waitt("vector", last_vstat)
        tk1 = S.emit("vector", lambda e: e.tensor_scalar(out=stt_m[:], in0=stt_sum[:],
                                                         scalar1=1.0 / DM, scalar2=None,
                                                         op0=OP.mult), inc="cV")
        tk2 = S.emit("vector", lambda e: e.tensor_scalar(out=stt_v[:], in0=stt_sq[:],
                                                         scalar1=1.0 / DM, scalar2=None,
                                                         op0=OP.mult), inc="cV")
        S.waitt("vector", tk1)
        tk3 = S.emit("vector", lambda e: e.tensor_tensor(out=stt_t[:], in0=stt_m[:],
                                                         in1=stt_m[:], op=OP.mult), inc="cV")
        S.waitt("vector", tk2)
        S.waitt("vector", tk3)
        tv = S.emit("vector", lambda e: e.tensor_tensor(out=stt_v[:], in0=stt_v[:],
                                                        in1=stt_t[:], op=OP.subtract), inc="cV")
        S.waitt("scalar", tv)
        S.waitt("scalar", tok_consts)
        tsq = S.emit("scalar", lambda e: e.activation(out=stt_t[:], in_=stt_v[:],
                                                      func=AF.Sqrt, bias=eps_ap[:]), inc="cS")
        S.waitt("vector", tsq)
        tk4 = S.emit("vector", lambda e: e.reciprocal(out=stt_rs[:], in_=stt_t[:]), inc="cV")
        S.waitt("vector", tk4)
        tk5 = S.emit("vector", lambda e: e.tensor_tensor(out=stt_nb[:], in0=stt_m[:],
                                                         in1=stt_rs[:], op=OP.mult), inc="cV")
        S.waitt("vector", tk5)
        tnb = S.emit("vector", lambda e: e.tensor_scalar(out=stt_nb[:], in0=stt_nb[:],
                                                         scalar1=-1.0, scalar2=None,
                                                         op0=OP.mult), inc="cV")

        w_in_bf = A(768, 16384).bitcast(BF16).rearrange("p (g t) -> p g t", g=GC)
        w_stage = [A(o, 2048) for o in (35592, 37640, 39688, 41736, 30464, 32512)]
        NSL = 6
        tok_stage = [None] * NSL
        tok_cast = [None] * NSL

        def stage_load(u):
            half, kc = u % 2, u // 2
            sl = u % NSL
            if u < 2:
                S.waitt("gpsimd", tok_wxp)
                S.waitt("gpsimd", tok_wdt)
            S.waitt("gpsimd", tok_cast[sl])
            tok_stage[sl] = S.dma("gpsimd", w_stage[sl],
                                  w_in_d[:, kc, half * 2048:(half + 1) * 2048], "dG")

        def stage_cast(u):
            half, kc = u % 2, u // 2
            sl = u % NSL
            S.waitt("scalar", tok_stage[sl])
            tok_cast[sl] = S.emit("scalar", lambda e, kc=kc, half=half, sl=sl: e.copy(
                out=w_in_bf[:, kc, half * 2048:(half + 1) * 2048],
                in_=w_stage[sl]), inc="cS")

        for u_ in range(NSL):
            stage_load(u_)

        tr_tok = [None, None]
        ev_tok = [None, None]
        cg_tok = [None, None]
        bn_dma = [None, None]
        tok_xn_w = []
        ta = None
        for i in range(NT):
            sl = i % 2
            S.waitt("vector", tnb)
            S.waitt("vector", tr_tok[sl])
            tp1 = S.emit("vector", lambda e, i=i, sl=sl: e.tensor_scalar(
                out=xn_tok[sl][:], in0=x_full[:, i * 1024:(i + 1) * 1024],
                scalar1=stt_rs[:, i:i + 1], scalar2=None, op0=OP.mult), inc="cV")
            S.waitt("vector", tp1)
            ta = S.emit("vector", lambda e, i=i, sl=sl: e.tensor_scalar(
                out=xn_tok[sl][:], in0=xn_tok[sl][:],
                scalar1=stt_nb[:, i:i + 1], scalar2=None, op0=OP.add), inc="cV")
            S.waitt("tensor", ta)
            S.waitt("tensor", tok_id)
            S.waitt("tensor", ev_tok[sl])
            S.waitt("tensor", cg_tok[sl])
            tt = None
            for gc in range(GC):
                tt = S.emit("tensor", lambda e, gc=gc, sl=sl: e.transpose(
                    ps_half[sl][:, gc * 128:(gc + 1) * 128],
                    xn_tok[sl][:, gc * 128:(gc + 1) * 128], id_f32[:]),
                    inc="cT" if gc == GC - 1 else None)
            tr_tok[sl] = tt
            S.waitt("scalar", tt)
            S.waitt("scalar", bn_dma[sl])
            S.waitt("scalar", cg_tok[sl])
            ev_tok[sl] = S.emit("scalar", lambda e, sl=sl: e.copy(
                out=xn_bnc[sl][:], in_=ps_half[sl]), inc="cS")
            cg_tok[sl] = S.emit("scalar", lambda e, i=i, sl=sl: e.copy(
                out=xn_bf_full.rearrange("p (g t) -> p g t", g=GC)[:, :, i * 128:(i + 1) * 128],
                in_=ps_half[sl][:].rearrange("p (g t) -> p g t", g=GC)), inc="cS")
            S.waitt("sync", ev_tok[sl])
            bn_dma[sl] = S.dma("sync", s_xn[:, :, i * 128:(i + 1) * 128],
                               xn_bnc[sl][:].rearrange("p (g t) -> p g t", g=GC), "dS")
            tok_xn_w.append(bn_dma[sl])
            S.waitt("scalar", ta)
            stage_cast(i)
            if i + NSL < 2 * GC:
                stage_load(i + NSL)
        tok_cast_h = [tok_cast[2], tok_cast[3]]

        xb_win = [A(25344, 2052), A(27396, 2052)]
        xc_b = [A(29448, 2048), A(31496, 2048)]
        xcbf_b = [A(33544, 1024).bitcast(BF16), A(34568, 1024).bitcast(BF16)]

        S.waitt("vector", tr_tok[0])
        S.waitt("vector", tr_tok[1])
        S.waitt("vector", last_sstat)
        S.emit("vector", lambda e: e.memset(xb_win[0][:, 0:3], 0.0))
        tvz = S.emit("vector", lambda e: e.memset(xb_win[1][:, 0:3], 0.0), inc="cV")

        tok_xc_w = {}
        tok_xcr = [None, None]
        tok_xc_dma = [None, None]
        tok_projslot = [None, None]
        tok_win_free = [None, None]
        tok_tk = {}
        tok_xcrg = {}
        tok_proj = None

        def do_conv(g):
            sl = g % 2
            if g == 0:
                S.waitt("vector", tok_cast[4])
                S.waitt("vector", bn_dma[0])
                S.waitt("vector", bn_dma[1])
            if g == 1:
                S.waitt("vector", tok_cast[4])
                S.waitt("vector", tok_cast[5])
            S.waitt("vector", tok_xb_w[(g, 0)])
            S.waitt("vector", tok_xb_w[(g, 1)])
            S.waitt("vector", tok_xcr[sl])
            S.waitt("vector", tok_xc_dma[sl])
            t0 = S.emit("vector", lambda e, g=g, sl=sl: e.tensor_scalar(
                out=xc_b[sl][:], in0=xb_win[sl][:, 0:2048],
                scalar1=conv_w[:, 4 * g:4 * g + 1], scalar2=None,
                op0=OP.mult), inc="cV")
            tk = t0
            for k in (1, 2, 3):
                S.waitt("vector", tk)
                tk = S.emit("vector", lambda e, g=g, sl=sl, k=k: e.scalar_tensor_tensor(
                    out=xc_b[sl][:], in0=xb_win[sl][:, k:k + 2048],
                    scalar=conv_w[:, 4 * g + k:4 * g + k + 1], in1=xc_b[sl][:],
                    op0=OP.mult, op1=OP.add), inc="cV")
            tok_win_free[sl] = tk
            tok_tk[g] = tk

        def do_finish(g):
            sl = g % 2
            S.waitt("scalar", tok_tk[g])
            tsl = S.emit("scalar", lambda e, g=g, sl=sl: e.activation(
                out=xc_b[sl][:], in_=xc_b[sl][:], func=AF.Silu,
                bias=conv_b[:, g:g + 1]), inc="cS")
            S.waitt("vector", tsl)
            S.waitt("vector", tok_projslot[sl])
            if g == 0:
                S.waitt("vector", tok_cast[5])
            tcr = S.emit("vector", lambda e, sl=sl: e.tensor_scalar(
                out=xcbf_b[sl][:], in0=xc_b[sl][:], scalar1=1.0, scalar2=None,
                op0=OP.mult), inc="cV")
            tok_xcr[sl] = tcr
            tok_xcrg[g] = tcr
            S.waitt("sync", tsl)
            tok_xc_w[g] = S.dma("sync", s_xc[:, g, :], xc_b[sl][:], "dS")
            tok_xc_dma[sl] = tok_xc_w[g]

        def do_proj(g):
            nonlocal tok_proj
            sl = g % 2
            S.waitt("tensor", tok_xcrg[g])
            S.waitt("tensor", tok_wxp)
            if g == 0:
                S.waitt("tensor", ev_tok[0])
                S.waitt("tensor", ev_tok[1])
                S.waitt("tensor", cg_tok[0])
                S.waitt("tensor", cg_tok[1])
            for q in range(4):
                tok_proj = S.emit("tensor", lambda e, g=g, sl=sl, q=q: e.matmul(
                    psA[0:96, q * 512:(q + 1) * 512], w_xp[:, g, :],
                    xcbf_b[sl][:, q * 512:(q + 1) * 512],
                    start=(g == 0), stop=(g == NG - 1)),
                    inc="cT" if q == 3 else None)
            tok_projslot[sl] = tok_proj

        mm_bnc = [A(39688, 1024), A(41736, 1024)]
        xn_bf = xn_bf_full.rearrange("p (g t) -> p g t", g=GC)

        tok_evac = [None, None]
        tok_bnc_dma = [None, None]
        tok_xb_w = {}
        tok_sz_w = {}
        pc_src = [A(35592, 1024), A(37640, 1024)]
        pc_dst = [A(36616, 512).bitcast(BF16), A(38664, 512).bitcast(BF16)]
        tok_pc_ld = [None, None]
        tok_pc_cast = [None, None]
        tok_pc_st = [None, None]

        def pc_unit(u):
            sl = u % 2
            if u < 2:
                S.waitt("gpsimd", tok_cast[0])
                S.waitt("gpsimd", tok_cast[1])
            S.waitt("gpsimd", tok_pc_cast[sl])
            tok_pc_ld[sl] = S.dma("gpsimd", pc_src[sl][:], w_out_d[:, u, :], "dG")
            S.waitt("scalar", tok_pc_ld[sl])
            S.waitt("scalar", tok_pc_st[sl])
            tok_pc_cast[sl] = S.emit("scalar", lambda e, sl=sl: e.copy(
                out=pc_dst[sl][:], in_=pc_src[sl][:]), inc="cS")
            S.waitt("gpsimd", tok_pc_cast[sl])
            tok_pc_st[sl] = S.dma("gpsimd", s_wq[:, u, :], pc_dst[sl][:], "dG")

        S.waitt("tensor", cg_tok[0])
        S.waitt("tensor", cg_tok[1])
        for m in range(16):
            for half in range(2):
                for hh in range(2):
                    slu = (half * 2 + hh) % 2
                    S.waitt("tensor", tok_cast_h[half])
                    S.waitt("tensor", tok_evac[slu])
                    tmm = None
                    for q in range(2):
                        col = slu * 1024 + q * 512
                        tc0 = hh * 1024 + q * 512
                        for kc in range(GC):
                            tmm = S.emit("tensor", lambda e, kc=kc, m=m, half=half,
                                         col=col, tc0=tc0: e.matmul(
                                psB[:, col:col + 512],
                                w_in_bf[:, kc, half * 2048 + m * 128:half * 2048 + m * 128 + 128],
                                xn_bf[:, kc, tc0:tc0 + 512],
                                start=(kc == 0), stop=(kc == GC - 1)),
                                inc="cT" if (kc == GC - 1 and q == 1) else None)
                    S.waitt("scalar", tmm)
                    if half == 0:
                        sl = m % 2
                        if m == 1:
                            S.waitt("scalar", bn_dma[0])
                            S.waitt("scalar", bn_dma[1])
                        S.waitt("scalar", tvz)
                        S.waitt("scalar", tok_win_free[sl])
                        tok_evac[slu] = S.emit("scalar", lambda e, hh=hh, sl=sl, slu=slu: e.copy(
                            out=xb_win[sl][:, 3 + hh * 1024:3 + hh * 1024 + 1024],
                            in_=psB[:, slu * 1024:slu * 1024 + 1024]), inc="cS")
                        tok_xb_w[(m, hh)] = tok_evac[slu]
                    else:
                        S.waitt("scalar", tok_bnc_dma[hh])
                        tok_evac[slu] = S.emit("scalar", lambda e, hh=hh, slu=slu: e.activation(
                            out=mm_bnc[hh][:], in_=psB[:, slu * 1024:slu * 1024 + 1024],
                            func=AF.Silu), inc="cS")
                        S.waitt("sync", tok_evac[slu])
                        mk = S.dma("sync", s_sz[:, m, hh * 1024:hh * 1024 + 1024],
                                   mm_bnc[hh][:], "dS", mark=(hh == 1))
                        if mk is not None:
                            tok_bnc_dma[0] = tok_bnc_dma[1] = mk
                        tok_sz_w[(m, hh)] = mk
            do_conv(m)
            if m >= 1:
                do_finish(m - 1)
            if m >= 2:
                do_proj(m - 2)
            pc_unit(m)
        do_finish(15)
        do_proj(14)
        do_proj(15)
        tok_sz_last = tok_bnc_dma[1]

        dr = A(45576, 1024, slice(0, DTR)).bitcast(BF16)
        dl_b = [A(35592, 2048), A(37640, 2048)]
        tmpA = A(39688, 2048)
        tmpB = A(41736, 2048)
        bc_bnc = A(41736, 2048, slice(DTR, 96))

        S.waitt("scalar", tok_par_last)
        S.waitt("scalar", tok_wxp)
        S.waitt("scalar", tok_wdt)
        S.waitt("scalar", tok_bnc_dma[0])
        S.waitt("scalar", tok_bnc_dma[1])
        tA1 = S.emit("scalar", lambda e: e.activation(out=tmpA[:, 0:256], in_=a_scale[:],
                                                      func=AF.Exp), inc="cS")
        S.waitt("vector", tA1)
        tA2 = S.emit("vector", lambda e: e.tensor_scalar(out=a_scale[:], in0=tmpA[:, 0:256],
                                                         scalar1=-1.0, scalar2=None,
                                                         op0=OP.mult), inc="cV")

        S.waitt("vector", tok_proj)
        tdr = S.emit("vector", lambda e: e.tensor_scalar(
            out=dr[:], in0=psA[0:DTR, :], scalar1=1.0, scalar2=None, op0=OP.mult), inc="cV")
        S.waitt("scalar", tok_proj)
        tbc = S.emit("scalar", lambda e: e.copy(out=bc_bnc[:], in_=psA[DTR:96, :]), inc="cS")
        S.waitt("sync", tbc)
        tok_bc_w = S.dma("sync", s_bc[:, :], bc_bnc[:], "dA")

        tok_dl_w = {}
        tok_dl_dma = [None, None]
        tok_sp_s = [None, None]
        tok_sp_v = [None, None]
        ps_d = [psA, psB]
        tmpA4 = [A(39688, 2048), A(30464, 2048), A(25344, 2048), A(27392, 2048)]
        tmpB2 = [A(41736, 2048), A(32512, 2048)]
        tok_t5_a = [None, None, None, None]
        tok_t5_b = [None, None]
        tok_dm = {}
        tok_t1 = {}
        tok_t2 = {}
        tok_t3 = {}
        tok_t4 = {}

        def d_mm(g):
            sl = g % 2
            pp = ps_d[sl]
            S.waitt("tensor", tdr)
            if g == 0:
                S.waitt("tensor", tok_bc_w)
            if g == 1:
                S.waitt("tensor", tok_evac[0])
                S.waitt("tensor", tok_evac[1])
            S.waitt("tensor", tok_sp_s[sl])
            S.waitt("tensor", tok_sp_v[sl])
            S.waitt("tensor", tok_wdt)
            tdm = None
            for q in range(4):
                tdm = S.emit("tensor", lambda e, g=g, pp=pp, q=q: e.matmul(
                    pp[:, q * 512:(q + 1) * 512], w_dt[:, g * 128:(g + 1) * 128],
                    dr[:, q * 512:(q + 1) * 512], start=True, stop=True),
                    inc="cT" if q == 3 else None)
            tok_dm[g] = tdm

        def d_s1(g):
            sl, s4 = g % 2, g % 4
            pp = ps_d[sl]
            S.waitt("scalar", tok_dm[g])
            if g == 1:
                S.waitt("scalar", tok_xc_dma[0])
                S.waitt("scalar", tok_xc_dma[1])
            if g == 2:
                S.waitt("scalar", tok_tk[15])
            S.waitt("scalar", tok_t5_a[s4])
            tok_t1[g] = S.emit("scalar", lambda e, g=g, pp=pp, s4=s4: e.activation(
                out=tmpA4[s4][:], in_=pp[:], func=AF.Abs, bias=b_dt[:, g:g + 1]),
                inc="cS")
            tok_sp_s[sl] = tok_t1[g]

        def d_s2(g):
            sl, s4 = g % 2, g % 4
            S.waitt("scalar", tok_t1[g])
            if g == 0:
                S.waitt("scalar", tok_bc_w)
            S.waitt("scalar", tok_t5_b[sl])
            tok_t2[g] = S.emit("scalar", lambda e, sl=sl, s4=s4: e.activation(
                out=tmpB2[sl][:], in_=tmpA4[s4][:], func=AF.Exp, scale=neg1_ap[:]),
                inc="cS")

        def d_s3(g):
            sl, s4 = g % 2, g % 4
            S.waitt("scalar", tok_t2[g])
            tok_t3[g] = S.emit("scalar", lambda e, sl=sl, s4=s4: e.activation(
                out=tmpA4[s4][:], in_=tmpB2[sl][:], func=AF.Ln, bias=1.0), inc="cS")

        def d_v4(g):
            sl = g % 2
            pp = ps_d[sl]
            S.waitt("vector", tok_dm[g])
            S.waitt("vector", tok_t3[g])
            tok_t4[g] = S.emit("vector", lambda e, g=g, pp=pp, sl=sl: e.tensor_scalar(
                out=tmpB2[sl][:], in0=pp[:], scalar1=b_dt[:, g:g + 1], scalar2=0.0,
                op0=OP.add, op1=OP.max), inc="cV")
            tok_sp_v[sl] = tok_t4[g]

        def d_v5(g):
            sl, s4 = g % 2, g % 4
            if g <= 1:
                S.waitt("vector", tok_pc_st[0])
                S.waitt("vector", tok_pc_st[1])
            S.waitt("vector", tok_t4[g])
            S.waitt("vector", tok_t3[g])
            S.waitt("vector", tok_dl_dma[sl])
            S.waitt("vector", tok_wxp)
            S.waitt("vector", tok_wdt)
            t5 = S.emit("vector", lambda e, sl=sl, s4=s4: e.tensor_tensor(
                out=dl_b[sl][:], in0=tmpA4[s4][:], in1=tmpB2[sl][:], op=OP.add),
                inc="cV")
            tok_t5_a[s4] = tok_t5_b[sl] = t5
            S.waitt("sync", t5)
            tdm = S.dma("sync", s_dl[:, g, :], dl_b[sl][:], "dS", mark=(g % 2 == 1))
            tok_dl_w[g] = tdm
            if tdm is not None:
                tok_dl_w[g - 1] = tdm
                tok_dl_dma[0] = tok_dl_dma[1] = tdm

        for gp in range(0, NG, 2):
            d_mm(gp)
            d_mm(gp + 1)
            d_s1(gp)
            d_s1(gp + 1)
            d_s2(gp)
            d_s2(gp + 1)
            d_s3(gp)
            d_s3(gp + 1)
            d_v4(gp)
            d_v4(gp + 1)
            d_v5(gp)
            d_v5(gp + 1)


        st_dl = [A(768 + 2048 * i, 2048) for i in range(4)]
        st_xc = [A(8960 + 2048 * i, 2048) for i in range(4)]
        u_b = [A(17152, 2048), A(19200, 2048)]
        ab_b = [A(21248, 2048), A(23296, 2048)]
        bx_b = [A(25344, 2048), A(27392, 2048)]
        hs_b = [A(29440, 2048), A(31488, 2048)]
        p_b = [arF[:, 0:2048], arF[:, 2048:4096]]
        bbc_b = [A(33536, 2048), A(35584, 2048)]
        cbc_b = [A(37632, 2048), A(39680, 2048)]
        sz_b = [A(41728, 2048), A(43776, 2048)]
        yb_b = A(45824, 1024).bitcast(BF16)
        psY = [psA, psB]

        NB = NG // 2
        NSTEP = 32 * NB

        S.waitt("tensor", tok_sp_s[0])
        S.waitt("tensor", tok_sp_s[1])
        S.waitt("tensor", tok_sp_v[0])
        S.waitt("tensor", tok_sp_v[1])
        S.waitt("tensor", tok_idr)
        S.waitt("scalar", tA2)
        for _sl4 in range(NSL):
            S.waitt("sync", tok_cast[_sl4])

        tok_strip = {}
        tok_u = {}
        tok_bc = {}
        tok_sz = {}
        tokE = {}
        tokB = {}
        tokSc = {}
        tokY = {}
        tok_ybf_dma = None

        def p2_strips(blk):
            par = blk % 2
            S.wait("sync", "uYF", 2 * (blk - 1))
            for gi in range(2):
                g = 2 * blk + gi
                S.waitt("sync", tok_dl_w[g])
                S.waitt("sync", tok_xc_w[g])
                S.dma("sync", st_dl[par * 2 + gi], s_dl[:, g, :], "dS", mark=False)
                S.dma("sync", st_xc[par * 2 + gi], s_xc[:, g, :], "dS", mark=False)
            if blk == 0:
                S.waitt("sync", tok_sz_last)
                S.waitt("sync", tok_bc_w)
            tok_strip[blk] = S.marker("sync", "dS")

        def p2_sz(blk):
            S.waitt("sync", tok_sz_last)
            S.wait("sync", "uYF", 2 * blk)
            for gi in range(2):
                S.dma("sync", sz_b[gi], s_sz[:, 2 * blk + gi, :], "dS", mark=False)
            tok_sz[blk] = S.marker("sync", "dS")

        def p2_u(blk):
            par = blk % 2
            for gi in range(2):
                S.waitt("vector", tok_strip[blk])
                S.wait("vector", "uAB", 32 * blk)
                S.wait("vector", "uYF", 2 * blk)
                tok_u[(blk, gi)] = S.emit("vector", lambda e, par=par, gi=gi:
                                          e.tensor_tensor(out=u_b[gi][:],
                                                          in0=st_dl[par * 2 + gi][:],
                                                          in1=st_xc[par * 2 + gi][:],
                                                          op=OP.mult), inc="cV")

        def p2_bc(blk, s):
            bs = s % 2
            S.waitt("sync", tok_bc_w)
            base = 32 * blk
            need = base + 2 * (s - 2) + 2 if s >= 2 else base
            S.wait("sync", "uAB", need)
            S.wait("sync", "uYM", need)
            S.dma("sync", bbc_b[bs], s_bc[s:s + 1, :].partition_broadcast(P), "dS",
                  mark=False)
            S.dma("sync", cbc_b[bs], s_bc[DS + s:DS + s + 1, :].partition_broadcast(P),
                  "dS", mark=False)
            tok_bc[(blk, s)] = S.marker("sync", "dS")

        def p2_exp(k):
            blk, s, gi = k // 32, (k % 32) // 2, k % 2
            g = 2 * blk + gi
            par = blk % 2
            S.wait("scalar", "uAB", k - 1)
            S.waitt("scalar", tok_strip[blk])
            tokE[k] = S.emit("scalar", lambda e, g=g, gi=gi, par=par, sl=k % 2, s=s:
                             e.activation(out=ab_b[sl][:], in_=st_dl[par * 2 + gi][:],
                                          func=AF.Exp,
                                          scale=a_scale[:, g * 16 + s:g * 16 + s + 1]),
                             inc="cS")

        def p2_bx(k):
            blk, s, gi = k // 32, (k % 32) // 2, k % 2
            S.waitt("vector", tok_bc[(blk, s)])
            S.waitt("vector", tok_u[(blk, gi)])
            tokB[k] = S.emit("vector", lambda e, gi=gi, sl=k % 2, bs=s % 2: e.tensor_tensor(
                out=bx_b[sl][:], in0=u_b[gi][:], in1=bbc_b[bs][:], op=OP.mult), inc="cV")

        def p2_scan(k):
            S.waitt("vector", tokE[k])
            S.waitt("vector", tokB[k])
            S.wait("vector", "uYM", k - 1)
            tokSc[k] = S.emit("vector", lambda e, sl=k % 2, hl=k % 2: e.tensor_tensor_scan(
                out=hs_b[hl][:], data0=ab_b[sl][:], data1=bx_b[sl][:],
                initial=0.0, op0=OP.mult, op1=OP.add), inc="uAB")

        def p2_ym(k):
            blk, s = k // 32, (k % 32) // 2
            S.waitt("vector", tokSc[k])
            S.waitt("vector", tok_bc[(blk, s)])
            S.wait("vector", "uP", k - 1)
            tokY[k] = S.emit("vector", lambda e, hl=k % 2, bs=s % 2, pl=k % 2:
                             e.tensor_tensor(out=p_b[pl][:], in0=hs_b[hl][:],
                                             in1=cbc_b[bs][:], op=OP.mult), inc="uYM")

        def p2_mm(k):
            blk, s, gi = k // 32, (k % 32) // 2, k % 2
            S.waitt("tensor", tokY[k])
            if s == 0:
                S.wait("tensor", "uYF", 2 * blk)
            for q in range(4):
                S.emit("tensor", lambda e, gi=gi, pl=k % 2, q=q, s=s: e.matmul(
                    psY[gi][:, q * 512:(q + 1) * 512], id_f32r[:],
                    p_b[pl][:, q * 512:(q + 1) * 512],
                    start=(s == 0), stop=(s == DS - 1)),
                    inc="uP" if q == 3 else None)

        def p2_finalize(blk):
            par = blk % 2
            for gi in range(2):
                g = 2 * blk + gi
                S.wait("vector", "uP", 32 * blk + 31 + gi)
                t1 = S.emit("vector", lambda e, g=g, gi=gi, par=par: e.scalar_tensor_tensor(
                    out=u_b[gi][:], in0=st_xc[par * 2 + gi][:], scalar=d_par[:, g:g + 1],
                    in1=psY[gi][:], op0=OP.mult, op1=OP.add), inc="cV")
                S.waitt("vector", tok_sz[blk])
                S.waitt("vector", tok_ybf[0])
                S.waitt("vector", t1)
                t2 = S.emit("vector", lambda e, gi=gi: e.tensor_tensor(
                    out=yb_b[:], in0=u_b[gi][:], in1=sz_b[gi][:], op=OP.mult), inc="uYF")
                S.waitt("sync", t2)
                tok_ybf[0] = S.dma("sync", s_y[:, g, :], yb_b[:], "dS")

        w_out_bf = A(768, 8192).bitcast(BF16).rearrange("p (g t) -> p g t", g=NG)
        w_r_bf = A(8960, 4096).bitcast(BF16).rearrange("p (g t) -> p g t", g=GC)
        w_k_bf = A(13056, 4096).bitcast(BF16).rearrange("p (g t) -> p g t", g=GC)
        w_v_bf = A(17152, 4096).bitcast(BF16).rearrange("p (g t) -> p g t", g=GC)
        ytile = A(21248, 4096).bitcast(BF16).rearrange("p (g t) -> p g t", g=NG)
        xn_win = A(25344, 4104).rearrange("p (g t) -> p g t", g=GC)
        x_tok3 = A(29448, 2048).rearrange("p (g t) -> p g t", g=2)
        tmp3 = A(31496, 4096)
        xr_bf = A(35592, 2048).bitcast(BF16).rearrange("p (g t) -> p g t", g=GC)
        xk_bf = A(37640, 2048).bitcast(BF16).rearrange("p (g t) -> p g t", g=GC)
        r_sb = A(39688, 4096)
        kact_bf = A(43784, 2048).bitcast(BF16).rearrange("p (g t) -> p g t", g=GC)
        out_tok = A(39688, 4096).rearrange("p (g t) -> p g t", g=4)

        tok_ybf = [None]
        for k0 in range(NSTEP):
            blk, s, gi = k0 // 32, (k0 % 32) // 2, k0 % 2
            if k0 % 32 == 0:
                if blk >= 1:
                    p2_finalize(blk - 1)
                if blk == 0:
                    p2_strips(0)
                p2_u(blk)
            if gi == 0:
                p2_bc(blk, s)
            p2_exp(k0)
            p2_bx(k0)
            p2_scan(k0)
            p2_ym(k0)
            p2_mm(k0)
            if k0 % 32 == 8 and blk + 1 < NB:
                p2_strips(blk + 1)
            if k0 % 32 == 24:
                p2_sz(blk)
        p2_finalize(NB - 1)

        S.wait("sync", "uAB", 32 * NB)
        S.wait("sync", "uYM", 32 * NB)

        S.waitt("sync", tok_ybf[0])
        S.waitt("sync", bn_dma[0])
        S.waitt("sync", bn_dma[1])
        S.dma("sync", ytile[:, :, :], s_y[:, :, 0:512], "dS", mark=False)
        S.dma("sync", xn_win[:, :, 1:513], s_xn[:, :, 0:512], "dS", mark=False)
        tyl0 = S.marker("sync", "dS")
        S.waitt("sync", tok_strip[7])
        S.waitt("sync", tok_pc_st[0])
        S.waitt("sync", tok_pc_st[1])
        tok_wout = S.dma("sync", w_out_bf[:, :, :], s_wq[:, :, :], "dS")

        stB = [A(29448, 1024), A(30472, 1024)]
        tokB3 = [None, None]

        def p3_castB(wbf, wd, kcs, gate_yf):
            last = None
            for kc in kcs:
                sl = kc % 2
                S.waitt("sync", tokB3[sl])
                tst = S.dma("sync", stB[sl], wd[:, kc, :], "dS")
                S.waitt("scalar", tst)
                if gate_yf:
                    S.wait("scalar", "uYF", 2 * NB)
                tokB3[sl] = last = S.emit("scalar", lambda e, wbf=wbf, kc=kc, sl=sl: e.copy(
                    out=wbf[:, kc, :], in_=stB[sl][:]), inc="cS")
            return last

        tok_wr = p3_castB(w_r_bf, w_r_d, range(GC), False)
        tok_wk = p3_castB(w_k_bf, w_k_d, range(GC), True)
        tok_wv = p3_castB(w_v_bf, w_v_d, range(GC), True)

        tmz = S.emit("vector", lambda e: e.memset(xn_win[:, :, 0:1], 0.0), inc="cV")
        S.waitt("tensor", tok_wout)
        S.wait("tensor", "uYF", 2 * NB)

        Tc3 = 512
        relu_buf = A(21248, 4096)
        out_tok3 = A(31496, 4096).rearrange("p (g t) -> p g t", g=4)
        psq = [psA[:, 0:1024], psA[:, 1024:2048], psB[:, 0:1024], psB[:, 1024:2048]]
        out_wv = [A(35592, 2048), A(37640, 2048)]
        tok_ow = [None, None]
        tok_tr_prev = None
        tok_chunk_c = None
        tok_f1a_prev = None
        tok_mix_done = None
        xslot_free = [None, None]
        tyl_next = txn_next = tyl0
        for ch in range(4):
            t0 = ch * Tc3
            tyl = txn = tyl_next

            S.waitt("tensor", tyl)
            tom_q = {}
            for half, pp in ((0, psA), (1, psB)):
                if ch > 0:
                    if half == 0:
                        S.waitt("tensor", tok_f1a_prev)
                    else:
                        S.waitt("tensor", tok_chunk_c)
                for mc in range(4):
                    m = half * 4 + mc
                    for kc in range(NG):
                        tom = S.emit("tensor", lambda e, m=m, mc=mc, kc=kc, pp=pp: e.matmul(
                            pp[:, mc * 512:(mc + 1) * 512],
                            w_out_bf[:, kc, m * 128:m * 128 + 128],
                            ytile[:, kc, :], start=(kc == 0), stop=(kc == NG - 1)),
                            inc="cT" if kc == NG - 1 else None)
                        if kc == NG - 1:
                            tom_q[(half, mc)] = tom
            S.waitt("scalar", tok_tr_prev)
            tev_q = {}
            for half, pp in ((0, psA), (1, psB)):
                for mc in range(4):
                    S.waitt("scalar", tom_q[(half, mc)])
                    tev_q[(half, mc)] = S.emit(
                        "scalar", lambda e, half=half, mc=mc, pp=pp: e.copy(
                            out=tmp3[:, half * 2048 + mc * 512:half * 2048 + mc * 512 + 512],
                            in_=pp[:, mc * 512:(mc + 1) * 512]), inc="cS")
            tev_a = tev_q[(0, 3)]
            tev_b = tev_q[(1, 3)]

            S.waitt("vector", txn)
            S.waitt("vector", tmz)
            S.waitt("vector", tok_chunk_c)
            S.waitt("vector", tok_ow[0])
            S.waitt("vector", tok_ow[1])
            tms = S.emit("vector", lambda e: e.tensor_tensor(
                out=r_sb[:].rearrange("p (g t) -> p g t", g=GC)[:, :, :],
                in0=xn_win[:, :, 1:513], in1=xn_win[:, :, 0:512], op=OP.subtract), inc="cV")
            S.waitt("vector", tms)
            tmr = None
            for gc in range(GC):
                tmr = S.emit("vector", lambda e, gc=gc: e.scalar_tensor_tensor(
                    out=xr_bf[:, gc, :], in0=r_sb[:, gc * 512:(gc + 1) * 512],
                    scalar=mu_r[:, gc:gc + 1], in1=xn_win[:, gc, 0:512],
                    op0=OP.mult, op1=OP.add), inc="cV" if gc == GC - 1 else None)
            tmixl = None
            for gc in range(GC):
                tmixl = S.emit("vector", lambda e, gc=gc: e.scalar_tensor_tensor(
                    out=xk_bf[:, gc, :], in0=r_sb[:, gc * 512:(gc + 1) * 512],
                    scalar=mu_k[:, gc:gc + 1], in1=xn_win[:, gc, 0:512],
                    op0=OP.mult, op1=OP.add), inc="cV" if gc == GC - 1 else None)
            tok_mix_done = tmixl

            sig_q = {}
            for half, pp in ((0, psA), (1, psB)):
                S.waitt("tensor", tok_wr)
                S.waitt("tensor", tmr)
                for mc in range(4):
                    m = half * 4 + mc
                    S.waitt("tensor", tev_q[(half, mc)])
                    trm = None
                    for kc in range(GC):
                        trm = S.emit("tensor", lambda e, m=m, mc=mc, kc=kc, pp=pp: e.matmul(
                            pp[:, mc * 512:(mc + 1) * 512],
                            w_r_bf[:, kc, m * 128:m * 128 + 128],
                            xr_bf[:, kc, :], start=(kc == 0), stop=(kc == GC - 1)),
                            inc="cT" if kc == GC - 1 else None)
                    S.waitt("scalar", trm)
                    S.waitt("scalar", tmixl)
                    sig_q[(half, mc)] = S.emit(
                        "scalar", lambda e, half=half, mc=mc, pp=pp: e.activation(
                            out=r_sb[:, half * 2048 + mc * 512:half * 2048 + mc * 512 + 512],
                            in_=pp[:, mc * 512:(mc + 1) * 512], func=AF.Sigmoid), inc="cS")
            tok_sig = [sig_q[(0, 3)], sig_q[(1, 3)]]
            kq = {}
            for half, pp in ((0, psA), (1, psB)):
                S.waitt("tensor", tok_wk)
                S.waitt("tensor", tmixl)
                for mc in range(4):
                    m = half * 4 + mc
                    S.waitt("tensor", sig_q[(half, mc)])
                    tkm = None
                    for kc in range(GC):
                        tkm = S.emit("tensor", lambda e, m=m, mc=mc, kc=kc, pp=pp: e.matmul(
                            pp[:, mc * 512:(mc + 1) * 512],
                            w_k_bf[:, kc, m * 128:m * 128 + 128],
                            xk_bf[:, kc, :], start=(kc == 0), stop=(kc == GC - 1)),
                            inc="cT" if kc == GC - 1 else None)
                    S.waitt("scalar", tkm)
                    trl = S.emit("scalar", lambda e, half=half, mc=mc, pp=pp: e.activation(
                        out=relu_buf[:, half * 2048 + mc * 512:half * 2048 + mc * 512 + 512],
                        in_=pp[:, mc * 512:(mc + 1) * 512], func=AF.Relu), inc="cS")
                    S.waitt("scalar", trl)
                    kq[(half, mc)] = S.emit(
                        "scalar", lambda e, half=half, mc=mc: e.activation(
                            out=kact_bf.rearrange("p g t -> p (g t)")[
                                :, half * 2048 + mc * 512:half * 2048 + mc * 512 + 512],
                            in_=relu_buf[:, half * 2048 + mc * 512:half * 2048 + mc * 512 + 512],
                            func=AF.Square), inc="cS")
            tok_k = [kq[(0, 3)], kq[(1, 3)]]
            if ch < 3:
                t0n = (ch + 1) * Tc3
                S.waitt("sync", tok_k[0])
                S.waitt("sync", tok_k[1])
                S.waitt("sync", tok_ybf[0])
                S.dma("sync", ytile[:, :, :], s_y[:, :, t0n:t0n + Tc3], "dS", mark=False)
                S.waitt("sync", tok_mix_done)
                S.dma("sync", xn_win[:, :, :], s_xn[:, :, t0n - 1:t0n + Tc3], "dS",
                      mark=False)
                tyl_next = txn_next = S.marker("sync", "dS")
            tvm_q = {}
            for half, pp in ((0, psA), (1, psB)):
                S.waitt("tensor", tok_wv)
                S.waitt("tensor", tok_k[0])
                S.waitt("tensor", tok_k[1])
                for mc in range(4):
                    m = half * 4 + mc
                    for kc in range(GC):
                        tvm = S.emit("tensor", lambda e, m=m, mc=mc, kc=kc, pp=pp: e.matmul(
                            pp[:, mc * 512:(mc + 1) * 512],
                            w_v_bf[:, kc, m * 128:m * 128 + 128],
                            kact_bf[:, kc, :], start=(kc == 0), stop=(kc == GC - 1)),
                            inc="cT" if kc == GC - 1 else None)
                        if kc == GC - 1:
                            tvm_q[(half, mc)] = tvm
            tf1a = tf1b = None
            for half, pp in ((0, psA), (1, psB)):
                for mc in range(4):
                    S.waitt("vector", tvm_q[(half, mc)])
                    S.waitt("vector", sig_q[(half, mc)])
                    tq = S.emit("vector", lambda e, half=half, mc=mc, pp=pp: e.tensor_tensor(
                        out=r_sb[:, half * 2048 + mc * 512:half * 2048 + mc * 512 + 512],
                        in0=r_sb[:, half * 2048 + mc * 512:half * 2048 + mc * 512 + 512],
                        in1=pp[:, mc * 512:(mc + 1) * 512], op=OP.mult), inc="cV")
                    if half == 0:
                        tf1a = tq
                    else:
                        tf1b = tq
            S.waitt("vector", tf1a)
            S.waitt("vector", tev_a)
            tf2a = S.emit("vector", lambda e: e.tensor_tensor(
                out=tmp3[:, 0:2048], in0=tmp3[:, 0:2048], in1=r_sb[:, 0:2048], op=OP.add),
                inc="cV")
            S.waitt("vector", tf1b)
            S.waitt("vector", tev_b)
            tf2b = S.emit("vector", lambda e: e.tensor_tensor(
                out=tmp3[:, 2048:4096], in0=tmp3[:, 2048:4096], in1=r_sb[:, 2048:4096],
                op=OP.add), inc="cV")
            t_add = []
            for wave in range(2):
                S.waitt("tensor", tf2a)
                S.waitt("tensor", tf2b)
                S.waitt("tensor", tf1b)
                if wave == 1:
                    S.waitt("tensor", t_add[1])
                ttr = None
                for ti in range(2):
                    tsub = wave * 2 + ti
                    for gc in range(GC):
                        ttr = S.emit("tensor", lambda e, tsub=tsub, ti=ti, gc=gc: e.transpose(
                            psB[:, ti * 1024 + gc * 128:ti * 1024 + (gc + 1) * 128],
                            tmp3.rearrange("p (g t) -> p g t", g=GC)[:, gc,
                                                                    tsub * 128:(tsub + 1) * 128],
                            id_f32[:]),
                            inc="cT" if gc == GC - 1 else None)
                    for _ in (0,):
                        pass
                for ti in range(2):
                    tsub = wave * 2 + ti
                    sl = tsub % 2
                    if ch == 0 and tsub < 2:
                        S.waitt("sync", tok_wv)
                    S.waitt("sync", xslot_free[sl])
                    txx = S.dma("sync", x_tok3[:, sl, :],
                                x_d.rearrange("(n p) c -> p n c", p=P)[:, ch * 4 + tsub, :],
                                "dS")
                    S.waitt("vector", ttr)
                    S.waitt("vector", txx)
                    t_add.append(S.emit("vector", lambda e, wave=wave, sl=sl, ti=ti:
                                        e.tensor_tensor(
                        out=out_wv[wave][:, ti * 1024:ti * 1024 + 1024],
                        in0=x_tok3[:, sl, :],
                        in1=psB[:, ti * 1024:ti * 1024 + 1024], op=OP.add), inc="cV"))
                    xslot_free[sl] = t_add[-1]
                S.waitt("sync", t_add[-1])
                tok_ow[wave] = S.dma(
                    "sync",
                    out_d.rearrange("(n p) c -> p n c", p=P)[:,
                                                             ch * 4 + wave * 2:
                                                             ch * 4 + wave * 2 + 2, :],
                    out_wv[wave][:].rearrange("p (n c) -> p n c", n=2), "dS")
            tok_chunk_c = t_add[-1]
            tok_f1a_prev = tf1a
            tok_tr_prev = ttr

        S.wait("sync", "dS", S.cnt["dS"])
        S.wait("sync", "dA", S.cnt["dA"])

        def replay(eng_obj, name):
            for it in S.items[name]:
                if it[0] == "w":
                    eng_obj.wait_ge(sems[it[1]], it[2])
                elif it[0] == "i":
                    eng_obj.sem_inc(sems[it[1]], 1)
                else:
                    _, fn, inc, amt = it
                    inst = fn(eng_obj)
                    if inc is not None:
                        inst.then_inc(sems[inc], amt)

        @block.sync
        def _(eng):
            replay(eng, "sync")

        @block.tensor
        def _(eng):
            replay(eng, "tensor")

        @block.vector
        def _(eng):
            replay(eng, "vector")

        @block.scalar
        def _(eng):
            replay(eng, "scalar")

        @block.gpsimd
        def _(eng):
            replay(eng, "gpsimd")

    return nc


def _relayout(w, ng):
    return np.ascontiguousarray(np.asarray(w, np.float32).reshape(ng, P, -1).transpose(1, 0, 2))


def kernel(L0_attn_x, L0_attn_W_in, L0_attn_conv_w, L0_attn_conv_b, L0_attn_W_xproj,
           L0_attn_W_dt, L0_attn_b_dt, L0_attn_A_log, L0_attn_D_param, L0_attn_W_out,
           L0_ffn_W_r, L0_ffn_W_k, L0_ffn_W_v, L0_ffn_mu_r, L0_ffn_mu_k):
    x = np.ascontiguousarray(np.asarray(L0_attn_x, dtype=np.float32))
    ident = np.eye(P, dtype=np.float32)
    shared = {
        "w_in": _relayout(L0_attn_W_in, GC),
        "w_xp": _relayout(L0_attn_W_xproj, NG),
        "w_dt": np.ascontiguousarray(np.asarray(L0_attn_W_dt, np.float32)),
        "w_out": _relayout(L0_attn_W_out, NG),
        "w_r": _relayout(np.ascontiguousarray(np.asarray(L0_ffn_W_r, np.float32).T), GC),
        "w_k": _relayout(np.ascontiguousarray(np.asarray(L0_ffn_W_k, np.float32).T), GC),
        "w_v": _relayout(np.ascontiguousarray(np.asarray(L0_ffn_W_v, np.float32).T), GC),
        "a_log": _relayout(L0_attn_A_log, NG),
        "conv_w": _relayout(L0_attn_conv_w, NG),
        "conv_b": np.ascontiguousarray(np.asarray(L0_attn_conv_b, np.float32).reshape(NG, P).T),
        "b_dt": np.ascontiguousarray(np.asarray(L0_attn_b_dt, np.float32).reshape(NG, P).T),
        "d_par": np.ascontiguousarray(np.asarray(L0_attn_D_param, np.float32).reshape(NG, P).T),
        "mu_r": np.ascontiguousarray(np.asarray(L0_ffn_mu_r, np.float32).reshape(GC, P).T),
        "mu_k": np.ascontiguousarray(np.asarray(L0_ffn_mu_k, np.float32).reshape(GC, P).T),
        "ident": ident,
        "identr": ident,
    }
    nc = build_nc()
    in_maps = [dict(shared, x=x[b]) for b in range(8)]
    res = run_bass_kernel_spmd(nc, in_maps, core_ids=list(range(8)))
    out = np.stack([np.asarray(res.results[b]["out"]) for b in range(8)], axis=0)
    return out.astype(np.float32)


# version 39
# speedup vs baseline: 1.0089x; 1.0089x over previous
import contextlib
import numpy as np
import concourse.bass as bass
import concourse.mybir as mybir
from concourse.bass_utils import run_bass_kernel_spmd

F32 = mybir.dt.float32
F32R = mybir.dt.float32r
BF16 = mybir.dt.bfloat16
AF = mybir.ActivationFunctionType
OP = mybir.AluOpType
AX = mybir.AxisListType

P = 128
T = 2048
DM = 1024
DI = 2048
DS = 16
DTR = 64
NG = DI // P
GC = DM // P
NT = T // P
EPS = 1e-5
ARENA = 46848
ARENA_F32R = 4224

ENGINES = ("sync", "tensor", "vector", "scalar", "gpsimd")
SEMS = ("dS", "dA", "dG", "cV", "cS", "cT", "cG", "uAB", "uYM", "uP", "uYF", "dM", "dN", "dH")


class Sched:

    def __init__(self):
        self.items = {e: [] for e in ENGINES}
        self.cnt = {s: 0 for s in SEMS}
        self.mark = {e: {s: 0 for s in SEMS} for e in ENGINES}

    def wait(self, eng, sem, val):
        if val <= 0 or self.mark[eng][sem] >= val:
            return
        self.mark[eng][sem] = val
        self.items[eng].append(("w", sem, val))

    def waitt(self, eng, tok):
        if tok is not None:
            self.wait(eng, tok[0], tok[1])

    def emit(self, eng, fn, inc=None, amt=1):
        tok = None
        if inc is not None:
            self.cnt[inc] += amt
            tok = (inc, self.cnt[inc])
        self.items[eng].append(("e", fn, inc, amt))
        return tok

    def dma(self, q, out, in_, sem, mark=True):
        self.emit(q, lambda e, o=out, i=in_: e.dma_start(out=o, in_=i), inc=sem, amt=16)
        if mark:
            return self.marker(q, sem)
        return None

    def marker(self, q, dsem):
        msem = {"dS": "dM", "dG": "dH"}.get(dsem, "dN")
        self.wait(q, dsem, self.cnt[dsem])
        self.cnt[msem] += 1
        tok = (msem, self.cnt[msem])
        self.items[q].append(("i", msem))
        return tok


def build_nc():
    nc = bass.Bass("TRN2", target_bir_lowering=False)

    x_d = nc.dram_tensor("x", [T, DM], F32, kind="ExternalInput").ap()
    w_in_d = nc.dram_tensor("w_in", [P, GC, 2 * DI], F32, kind="ExternalInput").ap()
    w_xp_d = nc.dram_tensor("w_xp", [P, NG, 96], F32, kind="ExternalInput").ap()
    w_dt_d = nc.dram_tensor("w_dt", [DTR, DI], F32, kind="ExternalInput").ap()
    w_out_d = nc.dram_tensor("w_out", [P, NG, DM], F32, kind="ExternalInput").ap()
    w_r_d = nc.dram_tensor("w_r", [P, GC, DM], F32, kind="ExternalInput").ap()
    w_k_d = nc.dram_tensor("w_k", [P, GC, DM], F32, kind="ExternalInput").ap()
    w_v_d = nc.dram_tensor("w_v", [P, GC, DM], F32, kind="ExternalInput").ap()
    alog_d = nc.dram_tensor("a_log", [P, NG, DS], F32, kind="ExternalInput").ap()
    cw_d = nc.dram_tensor("conv_w", [P, NG, 4], F32, kind="ExternalInput").ap()
    cb_d = nc.dram_tensor("conv_b", [P, NG], F32, kind="ExternalInput").ap()
    bdt_d = nc.dram_tensor("b_dt", [P, NG], F32, kind="ExternalInput").ap()
    dpar_d = nc.dram_tensor("d_par", [P, NG], F32, kind="ExternalInput").ap()
    mur_d = nc.dram_tensor("mu_r", [P, GC], F32, kind="ExternalInput").ap()
    muk_d = nc.dram_tensor("mu_k", [P, GC], F32, kind="ExternalInput").ap()
    idr_d = nc.dram_tensor("identr", [P, P], F32R, kind="ExternalInput").ap()
    id_d = nc.dram_tensor("ident", [P, P], F32, kind="ExternalInput").ap()
    out_d = nc.dram_tensor("out", [T, DM], F32, kind="ExternalOutput").ap()

    s_xb = nc.dram_tensor("s_xb", [P, NG, T], F32, kind="Internal").ap()
    s_xc = nc.dram_tensor("s_xc", [P, NG, T], F32, kind="Internal").ap()
    s_dl = nc.dram_tensor("s_dl", [P, NG, T], F32, kind="Internal").ap()
    s_sz = nc.dram_tensor("s_sz", [P, NG, T], F32, kind="Internal").ap()
    s_xn = nc.dram_tensor("s_xn", [P, GC, T], F32, kind="Internal").ap()
    s_bc = nc.dram_tensor("s_bc", [2 * DS, T], F32, kind="Internal").ap()
    s_y = nc.dram_tensor("s_y", [P, NG, T], BF16, kind="Internal").ap()
    s_wq = nc.dram_tensor("s_wq", [P, 16, 1024], BF16, kind="Internal").ap()

    with contextlib.ExitStack() as stk:
        E = stk.enter_context
        ar = E(nc.sbuf_tensor("arena", [P, ARENA], F32))
        arF = E(nc.sbuf_tensor("arenaF", [P, ARENA_F32R], F32R))
        psA = E(nc.psum_tensor("psA", [P, 2048], F32))
        psB = E(nc.psum_tensor("psB", [P, 2048], F32))
        sems = {s: E(nc.semaphore(s)) for s in SEMS}
        block = E(nc.Block())

        S = Sched()

        def A(c0, n, rows=slice(0, P)):
            return ar[rows, c0:c0 + n]

        a_scale = A(0, 256)
        conv_w = A(256, 64)
        conv_b = A(320, 16)
        b_dt = A(336, 16)
        d_par = A(352, 16)
        mu_r = A(368, 8)
        mu_k = A(376, 8)
        id_f32 = A(384, 128)
        id_f32r = arF[:, 4096:4224]
        stt_sum = A(640, 16)
        stt_sq = A(656, 16)
        stt_m = A(672, 16)
        stt_v = A(688, 16)
        stt_rs = A(704, 16)
        stt_nb = A(720, 16)
        stt_t = A(736, 16)
        eps_ap = A(752, 1)
        neg1_ap = A(753, 1)

        S.emit("vector", lambda e: e.memset(eps_ap[:], EPS))
        tok_consts = S.emit("vector", lambda e: e.memset(neg1_ap[:], -1.0), inc="cV")
        tok_id = None
        tok_idr = None
        for dst, src in [(a_scale, alog_d), (conv_w, cw_d), (conv_b, cb_d),
                         (b_dt, bdt_d), (d_par, dpar_d), (mu_r, mur_d), (mu_k, muk_d)]:
            S.dma("sync", dst, src, "dS", mark=False)
        S.dma("sync", id_f32, id_d, "dS", mark=False)
        S.dma("sync", id_f32r, idr_d, "dS", mark=False)
        tok_par_last = tok_id = tok_idr = S.marker("sync", "dS")
        w_xp = A(43784, 768).bitcast(BF16).rearrange("p (g m) -> p g m", g=NG)
        w_dt = A(44552, 1024, slice(0, DTR)).bitcast(BF16)
        wxp_stage = A(35592, 1536)
        wdt_stage = A(37640, 2048, slice(0, DTR))
        t_ = S.dma("sync", wxp_stage, w_xp_d.rearrange("p g m -> p (g m)"), "dS")
        S.waitt("scalar", t_)
        tok_wxp = S.emit("scalar", lambda e: e.copy(
            out=A(43784, 768).bitcast(BF16), in_=wxp_stage), inc="cS")
        t_ = S.dma("sync", wdt_stage, w_dt_d, "dS")
        S.waitt("scalar", t_)
        tok_wdt = S.emit("scalar", lambda e: e.copy(
            out=A(44552, 1024, slice(0, DTR)).bitcast(BF16),
            in_=wdt_stage), inc="cS")

        x_full = A(768, NT * 1024)
        xn_bf_full = A(17152, 8192).bitcast(BF16)
        junk = A(25344, 1024)
        xn_tok = [A(26368, 1024), A(27392, 1024)]
        xn_bnc = [A(28416, 1024), A(29440, 1024)]
        ps_half = [psA[:, 0:1024], psA[:, 1024:2048]]

        tok_x = []
        for i in range(NT):
            S.dma("sync", x_full[:, i * 1024:(i + 1) * 1024],
                  x_d.rearrange("(n p) c -> n p c", p=P)[i], "dS", mark=False)
            if i % 4 == 3:
                mk = S.marker("sync", "dS")
                tok_x.extend([mk] * 4)
        last_vstat = None
        last_sstat = None
        for i in range(NT):
            S.waitt("vector", tok_x[i])
            last_vstat = S.emit("vector", lambda e, i=i: e.tensor_reduce(
                out=stt_sum[:, i:i + 1], in_=x_full[:, i * 1024:(i + 1) * 1024],
                axis=AX.X, op=OP.add), inc="cV")
            S.waitt("scalar", tok_x[i])
            S.waitt("scalar", last_sstat)
            last_sstat = S.emit("scalar", lambda e, i=i: e.activation(
                out=junk[:], in_=x_full[:, i * 1024:(i + 1) * 1024], func=AF.Square,
                accum_out=stt_sq[:, i:i + 1]), inc="cS")
        S.waitt("vector", last_sstat)
        S.waitt("vector", last_vstat)
        tk1 = S.emit("vector", lambda e: e.tensor_scalar(out=stt_m[:], in0=stt_sum[:],
                                                         scalar1=1.0 / DM, scalar2=None,
                                                         op0=OP.mult), inc="cV")
        tk2 = S.emit("vector", lambda e: e.tensor_scalar(out=stt_v[:], in0=stt_sq[:],
                                                         scalar1=1.0 / DM, scalar2=None,
                                                         op0=OP.mult), inc="cV")
        S.waitt("vector", tk1)
        tk3 = S.emit("vector", lambda e: e.tensor_tensor(out=stt_t[:], in0=stt_m[:],
                                                         in1=stt_m[:], op=OP.mult), inc="cV")
        S.waitt("vector", tk2)
        S.waitt("vector", tk3)
        tv = S.emit("vector", lambda e: e.tensor_tensor(out=stt_v[:], in0=stt_v[:],
                                                        in1=stt_t[:], op=OP.subtract), inc="cV")
        S.waitt("scalar", tv)
        S.waitt("scalar", tok_consts)
        tsq = S.emit("scalar", lambda e: e.activation(out=stt_t[:], in_=stt_v[:],
                                                      func=AF.Sqrt, bias=eps_ap[:]), inc="cS")
        S.waitt("vector", tsq)
        tk4 = S.emit("vector", lambda e: e.reciprocal(out=stt_rs[:], in_=stt_t[:]), inc="cV")
        S.waitt("vector", tk4)
        tk5 = S.emit("vector", lambda e: e.tensor_tensor(out=stt_nb[:], in0=stt_m[:],
                                                         in1=stt_rs[:], op=OP.mult), inc="cV")
        S.waitt("vector", tk5)
        tnb = S.emit("vector", lambda e: e.tensor_scalar(out=stt_nb[:], in0=stt_nb[:],
                                                         scalar1=-1.0, scalar2=None,
                                                         op0=OP.mult), inc="cV")

        w_in_bf = A(768, 16384).bitcast(BF16).rearrange("p (g t) -> p g t", g=GC)
        w_stage = [A(o, 2048) for o in (35592, 37640, 39688, 41736, 30464, 32512)]
        NSL = 6
        tok_stage = [None] * NSL
        tok_cast = [None] * NSL

        def stage_load(u):
            half, kc = u % 2, u // 2
            sl = u % NSL
            if u < 2:
                S.waitt("gpsimd", tok_wxp)
                S.waitt("gpsimd", tok_wdt)
            S.waitt("gpsimd", tok_cast[sl])
            tok_stage[sl] = S.dma("gpsimd", w_stage[sl],
                                  w_in_d[:, kc, half * 2048:(half + 1) * 2048], "dG")

        def stage_cast(u):
            half, kc = u % 2, u // 2
            sl = u % NSL
            S.waitt("scalar", tok_stage[sl])
            tok_cast[sl] = S.emit("scalar", lambda e, kc=kc, half=half, sl=sl: e.copy(
                out=w_in_bf[:, kc, half * 2048:(half + 1) * 2048],
                in_=w_stage[sl]), inc="cS")

        for u_ in range(NSL):
            stage_load(u_)

        tr_tok = [None, None]
        ev_tok = [None, None]
        cg_tok = [None, None]
        bn_dma = [None, None]
        tok_xn_w = []
        ta = None
        for i in range(NT):
            sl = i % 2
            S.waitt("vector", tnb)
            S.waitt("vector", tr_tok[sl])
            tp1 = S.emit("vector", lambda e, i=i, sl=sl: e.tensor_scalar(
                out=xn_tok[sl][:], in0=x_full[:, i * 1024:(i + 1) * 1024],
                scalar1=stt_rs[:, i:i + 1], scalar2=None, op0=OP.mult), inc="cV")
            S.waitt("vector", tp1)
            ta = S.emit("vector", lambda e, i=i, sl=sl: e.tensor_scalar(
                out=xn_tok[sl][:], in0=xn_tok[sl][:],
                scalar1=stt_nb[:, i:i + 1], scalar2=None, op0=OP.add), inc="cV")
            S.waitt("tensor", ta)
            S.waitt("tensor", tok_id)
            S.waitt("tensor", ev_tok[sl])
            S.waitt("tensor", cg_tok[sl])
            tt = None
            for gc in range(GC):
                tt = S.emit("tensor", lambda e, gc=gc, sl=sl: e.transpose(
                    ps_half[sl][:, gc * 128:(gc + 1) * 128],
                    xn_tok[sl][:, gc * 128:(gc + 1) * 128], id_f32[:]),
                    inc="cT" if gc == GC - 1 else None)
            tr_tok[sl] = tt
            S.waitt("scalar", tt)
            S.waitt("scalar", bn_dma[sl])
            S.waitt("scalar", cg_tok[sl])
            ev_tok[sl] = S.emit("scalar", lambda e, sl=sl: e.copy(
                out=xn_bnc[sl][:], in_=ps_half[sl]), inc="cS")
            cg_tok[sl] = S.emit("scalar", lambda e, i=i, sl=sl: e.copy(
                out=xn_bf_full.rearrange("p (g t) -> p g t", g=GC)[:, :, i * 128:(i + 1) * 128],
                in_=ps_half[sl][:].rearrange("p (g t) -> p g t", g=GC)), inc="cS")
            S.waitt("sync", ev_tok[sl])
            bn_dma[sl] = S.dma("sync", s_xn[:, :, i * 128:(i + 1) * 128],
                               xn_bnc[sl][:].rearrange("p (g t) -> p g t", g=GC), "dS")
            tok_xn_w.append(bn_dma[sl])
            S.waitt("scalar", ta)
            stage_cast(i)
            if i + NSL < 2 * GC:
                stage_load(i + NSL)
        tok_cast_h = [tok_cast[2], tok_cast[3]]

        xb_win = [A(25344, 2052), A(27396, 2052)]
        xc_b = [A(29448, 2048), A(31496, 2048)]
        xcbf_b = [A(33544, 1024).bitcast(BF16), A(34568, 1024).bitcast(BF16)]

        S.waitt("vector", tr_tok[0])
        S.waitt("vector", tr_tok[1])
        S.waitt("vector", last_sstat)
        S.emit("vector", lambda e: e.memset(xb_win[0][:, 0:3], 0.0))
        tvz = S.emit("vector", lambda e: e.memset(xb_win[1][:, 0:3], 0.0), inc="cV")

        tok_xc_w = {}
        tok_xcr = [None, None]
        tok_xc_dma = [None, None]
        tok_projslot = [None, None]
        tok_win_free = [None, None]
        tok_tk = {}
        tok_xcrg = {}
        tok_proj = None

        def do_conv(g):
            sl = g % 2
            if g == 0:
                S.waitt("vector", tok_cast[4])
                S.waitt("vector", bn_dma[0])
                S.waitt("vector", bn_dma[1])
            if g == 1:
                S.waitt("vector", tok_cast[4])
                S.waitt("vector", tok_cast[5])
            S.waitt("vector", tok_xb_w[(g, 0)])
            S.waitt("vector", tok_xb_w[(g, 1)])
            S.waitt("vector", tok_xcr[sl])
            S.waitt("vector", tok_xc_dma[sl])
            t0 = S.emit("vector", lambda e, g=g, sl=sl: e.tensor_scalar(
                out=xc_b[sl][:], in0=xb_win[sl][:, 0:2048],
                scalar1=conv_w[:, 4 * g:4 * g + 1], scalar2=None,
                op0=OP.mult), inc="cV")
            tk = t0
            for k in (1, 2, 3):
                S.waitt("vector", tk)
                tk = S.emit("vector", lambda e, g=g, sl=sl, k=k: e.scalar_tensor_tensor(
                    out=xc_b[sl][:], in0=xb_win[sl][:, k:k + 2048],
                    scalar=conv_w[:, 4 * g + k:4 * g + k + 1], in1=xc_b[sl][:],
                    op0=OP.mult, op1=OP.add), inc="cV")
            tok_win_free[sl] = tk
            tok_tk[g] = tk

        def do_finish(g):
            sl = g % 2
            S.waitt("scalar", tok_tk[g])
            tsl = S.emit("scalar", lambda e, g=g, sl=sl: e.activation(
                out=xc_b[sl][:], in_=xc_b[sl][:], func=AF.Silu,
                bias=conv_b[:, g:g + 1]), inc="cS")
            S.waitt("vector", tsl)
            S.waitt("vector", tok_projslot[sl])
            if g == 0:
                S.waitt("vector", tok_cast[5])
            tcr = S.emit("vector", lambda e, sl=sl: e.tensor_scalar(
                out=xcbf_b[sl][:], in0=xc_b[sl][:], scalar1=1.0, scalar2=None,
                op0=OP.mult), inc="cV")
            tok_xcr[sl] = tcr
            tok_xcrg[g] = tcr
            S.waitt("sync", tsl)
            tok_xc_w[g] = S.dma("sync", s_xc[:, g, :], xc_b[sl][:], "dS")
            tok_xc_dma[sl] = tok_xc_w[g]

        def do_proj(g):
            nonlocal tok_proj
            sl = g % 2
            S.waitt("tensor", tok_xcrg[g])
            S.waitt("tensor", tok_wxp)
            if g == 0:
                S.waitt("tensor", ev_tok[0])
                S.waitt("tensor", ev_tok[1])
                S.waitt("tensor", cg_tok[0])
                S.waitt("tensor", cg_tok[1])
            for q in range(4):
                tok_proj = S.emit("tensor", lambda e, g=g, sl=sl, q=q: e.matmul(
                    psA[0:96, q * 512:(q + 1) * 512], w_xp[:, g, :],
                    xcbf_b[sl][:, q * 512:(q + 1) * 512],
                    start=(g == 0), stop=(g == NG - 1)),
                    inc="cT" if q == 3 else None)
            tok_projslot[sl] = tok_proj

        mm_bnc = [A(39688, 1024), A(41736, 1024)]
        xn_bf = xn_bf_full.rearrange("p (g t) -> p g t", g=GC)

        tok_evac = [None, None]
        tok_bnc_dma = [None, None]
        tok_xb_w = {}
        tok_sz_w = {}
        pc_src = [A(35592, 1024), A(37640, 1024)]
        pc_dst = [A(36616, 512).bitcast(BF16), A(38664, 512).bitcast(BF16)]
        tok_pc_ld = [None, None]
        tok_pc_cast = [None, None]
        tok_pc_st = [None, None]

        def pc_unit(u):
            sl = u % 2
            if u < 2:
                S.waitt("gpsimd", tok_cast[0])
                S.waitt("gpsimd", tok_cast[1])
            S.waitt("gpsimd", tok_pc_cast[sl])
            tok_pc_ld[sl] = S.dma("gpsimd", pc_src[sl][:], w_out_d[:, u, :], "dG")
            S.waitt("scalar", tok_pc_ld[sl])
            S.waitt("scalar", tok_pc_st[sl])
            tok_pc_cast[sl] = S.emit("scalar", lambda e, sl=sl: e.copy(
                out=pc_dst[sl][:], in_=pc_src[sl][:]), inc="cS")
            S.waitt("gpsimd", tok_pc_cast[sl])
            tok_pc_st[sl] = S.dma("gpsimd", s_wq[:, u, :], pc_dst[sl][:], "dG")

        S.waitt("tensor", cg_tok[0])
        S.waitt("tensor", cg_tok[1])
        for m in range(16):
            for half in range(2):
                for hh in range(2):
                    slu = (half * 2 + hh) % 2
                    S.waitt("tensor", tok_cast_h[half])
                    S.waitt("tensor", tok_evac[slu])
                    tmm = None
                    for q in range(2):
                        col = slu * 1024 + q * 512
                        tc0 = hh * 1024 + q * 512
                        for kc in range(GC):
                            tmm = S.emit("tensor", lambda e, kc=kc, m=m, half=half,
                                         col=col, tc0=tc0: e.matmul(
                                psB[:, col:col + 512],
                                w_in_bf[:, kc, half * 2048 + m * 128:half * 2048 + m * 128 + 128],
                                xn_bf[:, kc, tc0:tc0 + 512],
                                start=(kc == 0), stop=(kc == GC - 1)),
                                inc="cT" if (kc == GC - 1 and q == 1) else None)
                    S.waitt("scalar", tmm)
                    if half == 0:
                        sl = m % 2
                        if m == 1:
                            S.waitt("scalar", bn_dma[0])
                            S.waitt("scalar", bn_dma[1])
                        S.waitt("scalar", tvz)
                        S.waitt("scalar", tok_win_free[sl])
                        tok_evac[slu] = S.emit("scalar", lambda e, hh=hh, sl=sl, slu=slu: e.copy(
                            out=xb_win[sl][:, 3 + hh * 1024:3 + hh * 1024 + 1024],
                            in_=psB[:, slu * 1024:slu * 1024 + 1024]), inc="cS")
                        tok_xb_w[(m, hh)] = tok_evac[slu]
                    else:
                        S.waitt("scalar", tok_bnc_dma[hh])
                        tok_evac[slu] = S.emit("scalar", lambda e, hh=hh, slu=slu: e.activation(
                            out=mm_bnc[hh][:], in_=psB[:, slu * 1024:slu * 1024 + 1024],
                            func=AF.Silu), inc="cS")
                        S.waitt("sync", tok_evac[slu])
                        mk = S.dma("sync", s_sz[:, m, hh * 1024:hh * 1024 + 1024],
                                   mm_bnc[hh][:], "dS", mark=(hh == 1))
                        if mk is not None:
                            tok_bnc_dma[0] = tok_bnc_dma[1] = mk
                        tok_sz_w[(m, hh)] = mk
            do_conv(m)
            if m >= 1:
                do_finish(m - 1)
            if m >= 2:
                do_proj(m - 2)
            pc_unit(m)
        do_finish(15)
        do_proj(14)
        do_proj(15)
        tok_sz_last = tok_bnc_dma[1]

        dr = A(45576, 1024, slice(0, DTR)).bitcast(BF16)
        dl_b = [A(35592, 2048), A(37640, 2048)]
        tmpA = A(39688, 2048)
        tmpB = A(41736, 2048)
        bc_bnc = A(41736, 2048, slice(DTR, 96))

        S.waitt("scalar", tok_par_last)
        S.waitt("scalar", tok_wxp)
        S.waitt("scalar", tok_wdt)
        S.waitt("scalar", tok_bnc_dma[0])
        S.waitt("scalar", tok_bnc_dma[1])
        tA1 = S.emit("scalar", lambda e: e.activation(out=tmpA[:, 0:256], in_=a_scale[:],
                                                      func=AF.Exp), inc="cS")
        S.waitt("vector", tA1)
        tA2 = S.emit("vector", lambda e: e.tensor_scalar(out=a_scale[:], in0=tmpA[:, 0:256],
                                                         scalar1=-1.0, scalar2=None,
                                                         op0=OP.mult), inc="cV")

        S.waitt("vector", tok_proj)
        tdr = S.emit("vector", lambda e: e.tensor_scalar(
            out=dr[:], in0=psA[0:DTR, :], scalar1=1.0, scalar2=None, op0=OP.mult), inc="cV")
        S.waitt("scalar", tok_proj)
        tbc = S.emit("scalar", lambda e: e.copy(out=bc_bnc[:], in_=psA[DTR:96, :]), inc="cS")
        S.waitt("sync", tbc)
        tok_bc_w = S.dma("sync", s_bc[:, :], bc_bnc[:], "dA")

        tok_dl_w = {}
        tok_dl_dma = [None, None]
        tok_sp_s = [None, None]
        tok_sp_v = [None, None]
        ps_d = [psA, psB]
        tmpA4 = [A(39688, 2048), A(30464, 2048), A(25344, 2048), A(27392, 2048)]
        tmpB2 = [A(41736, 2048), A(32512, 2048)]
        tok_t5_a = [None, None, None, None]
        tok_t5_b = [None, None]
        tok_dm = {}
        tok_t1 = {}
        tok_t2 = {}
        tok_t3 = {}
        tok_t4 = {}

        def d_mm(g):
            sl = g % 2
            pp = ps_d[sl]
            S.waitt("tensor", tdr)
            if g == 0:
                S.waitt("tensor", tok_bc_w)
            if g == 1:
                S.waitt("tensor", tok_evac[0])
                S.waitt("tensor", tok_evac[1])
            S.waitt("tensor", tok_sp_s[sl])
            S.waitt("tensor", tok_sp_v[sl])
            S.waitt("tensor", tok_wdt)
            tdm = None
            for q in range(4):
                tdm = S.emit("tensor", lambda e, g=g, pp=pp, q=q: e.matmul(
                    pp[:, q * 512:(q + 1) * 512], w_dt[:, g * 128:(g + 1) * 128],
                    dr[:, q * 512:(q + 1) * 512], start=True, stop=True),
                    inc="cT" if q == 3 else None)
            tok_dm[g] = tdm

        def d_s1(g):
            sl, s4 = g % 2, g % 4
            pp = ps_d[sl]
            S.waitt("scalar", tok_dm[g])
            if g == 1:
                S.waitt("scalar", tok_xc_dma[0])
                S.waitt("scalar", tok_xc_dma[1])
            if g == 2:
                S.waitt("scalar", tok_tk[15])
            S.waitt("scalar", tok_t5_a[s4])
            tok_t1[g] = S.emit("scalar", lambda e, g=g, pp=pp, s4=s4: e.activation(
                out=tmpA4[s4][:], in_=pp[:], func=AF.Abs, bias=b_dt[:, g:g + 1]),
                inc="cS")
            tok_sp_s[sl] = tok_t1[g]

        def d_s2(g):
            sl, s4 = g % 2, g % 4
            S.waitt("scalar", tok_t1[g])
            if g == 0:
                S.waitt("scalar", tok_bc_w)
            S.waitt("scalar", tok_t5_b[sl])
            tok_t2[g] = S.emit("scalar", lambda e, sl=sl, s4=s4: e.activation(
                out=tmpB2[sl][:], in_=tmpA4[s4][:], func=AF.Exp, scale=neg1_ap[:]),
                inc="cS")

        def d_s3(g):
            sl, s4 = g % 2, g % 4
            S.waitt("scalar", tok_t2[g])
            tok_t3[g] = S.emit("scalar", lambda e, sl=sl, s4=s4: e.activation(
                out=tmpA4[s4][:], in_=tmpB2[sl][:], func=AF.Ln, bias=1.0), inc="cS")

        def d_v4(g):
            sl = g % 2
            pp = ps_d[sl]
            S.waitt("vector", tok_dm[g])
            S.waitt("vector", tok_t3[g])
            tok_t4[g] = S.emit("vector", lambda e, g=g, pp=pp, sl=sl: e.tensor_scalar(
                out=tmpB2[sl][:], in0=pp[:], scalar1=b_dt[:, g:g + 1], scalar2=0.0,
                op0=OP.add, op1=OP.max), inc="cV")
            tok_sp_v[sl] = tok_t4[g]

        def d_v5(g):
            sl, s4 = g % 2, g % 4
            if g <= 1:
                S.waitt("vector", tok_pc_st[0])
                S.waitt("vector", tok_pc_st[1])
            S.waitt("vector", tok_t4[g])
            S.waitt("vector", tok_t3[g])
            S.waitt("vector", tok_dl_dma[sl])
            S.waitt("vector", tok_wxp)
            S.waitt("vector", tok_wdt)
            t5 = S.emit("vector", lambda e, sl=sl, s4=s4: e.tensor_tensor(
                out=dl_b[sl][:], in0=tmpA4[s4][:], in1=tmpB2[sl][:], op=OP.add),
                inc="cV")
            tok_t5_a[s4] = tok_t5_b[sl] = t5
            S.waitt("sync", t5)
            tdm = S.dma("sync", s_dl[:, g, :], dl_b[sl][:], "dS", mark=(g % 2 == 1))
            tok_dl_w[g] = tdm
            if tdm is not None:
                tok_dl_w[g - 1] = tdm
                tok_dl_dma[0] = tok_dl_dma[1] = tdm

        for gp in range(0, NG, 2):
            d_mm(gp)
            d_mm(gp + 1)
            d_s1(gp)
            d_s1(gp + 1)
            d_s2(gp)
            d_s2(gp + 1)
            d_s3(gp)
            d_s3(gp + 1)
            d_v4(gp)
            d_v4(gp + 1)
            d_v5(gp)
            d_v5(gp + 1)


        st_dl = [A(768 + 2048 * i, 2048) for i in range(4)]
        st_xc = [A(8960 + 2048 * i, 2048) for i in range(4)]
        u_b = [A(17152, 2048), A(19200, 2048)]
        ab_b = [A(21248, 2048), A(23296, 2048)]
        bx_b = [A(25344, 2048), A(27392, 2048)]
        hs_b = [A(29440, 2048), A(31488, 2048)]
        p_b = [arF[:, 0:2048], arF[:, 2048:4096]]
        bbc_b = [A(33536, 2048), A(35584, 2048)]
        cbc_b = [A(37632, 2048), A(39680, 2048)]
        sz_b = [A(41728, 2048), A(43776, 2048)]
        yb_b = A(45824, 1024).bitcast(BF16)
        psY = [psA, psB]

        NB = NG // 2
        NSTEP = 32 * NB

        S.waitt("tensor", tok_sp_s[0])
        S.waitt("tensor", tok_sp_s[1])
        S.waitt("tensor", tok_sp_v[0])
        S.waitt("tensor", tok_sp_v[1])
        S.waitt("tensor", tok_idr)
        S.waitt("scalar", tA2)
        for _sl4 in range(NSL):
            S.waitt("sync", tok_cast[_sl4])

        tok_strip = {}
        tok_u = {}
        tok_bc = {}
        tok_sz = {}
        tokE = {}
        tokB = {}
        tokSc = {}
        tokY = {}
        tok_ybf_dma = None

        def p2_strips(blk):
            par = blk % 2
            S.wait("sync", "uYF", 2 * (blk - 1))
            for gi in range(2):
                g = 2 * blk + gi
                S.waitt("sync", tok_dl_w[g])
                S.waitt("sync", tok_xc_w[g])
                S.dma("sync", st_dl[par * 2 + gi], s_dl[:, g, :], "dS", mark=False)
                S.dma("sync", st_xc[par * 2 + gi], s_xc[:, g, :], "dS", mark=False)
            if blk == 0:
                S.waitt("sync", tok_sz_last)
                S.waitt("sync", tok_bc_w)
            tok_strip[blk] = S.marker("sync", "dS")

        def p2_sz(blk):
            S.waitt("sync", tok_sz_last)
            S.wait("sync", "uYF", 2 * blk)
            for gi in range(2):
                S.dma("sync", sz_b[gi], s_sz[:, 2 * blk + gi, :], "dS", mark=False)
            tok_sz[blk] = S.marker("sync", "dS")

        def p2_u(blk):
            par = blk % 2
            for gi in range(2):
                S.waitt("vector", tok_strip[blk])
                S.wait("vector", "uAB", 32 * blk)
                S.wait("vector", "uYF", 2 * blk)
                tok_u[(blk, gi)] = S.emit("vector", lambda e, par=par, gi=gi:
                                          e.tensor_tensor(out=u_b[gi][:],
                                                          in0=st_dl[par * 2 + gi][:],
                                                          in1=st_xc[par * 2 + gi][:],
                                                          op=OP.mult), inc="cV")

        def p2_bc(blk, s):
            bs = s % 2
            S.waitt("sync", tok_bc_w)
            base = 32 * blk
            need = base + 2 * (s - 2) + 2 if s >= 2 else base
            S.wait("sync", "uAB", need)
            S.wait("sync", "uYM", need)
            S.dma("sync", bbc_b[bs], s_bc[s:s + 1, :].partition_broadcast(P), "dS",
                  mark=False)
            S.dma("sync", cbc_b[bs], s_bc[DS + s:DS + s + 1, :].partition_broadcast(P),
                  "dS", mark=False)
            tok_bc[(blk, s)] = S.marker("sync", "dS")

        def p2_exp(k):
            blk, s, gi = k // 32, (k % 32) // 2, k % 2
            g = 2 * blk + gi
            par = blk % 2
            S.wait("scalar", "uAB", k - 1)
            S.waitt("scalar", tok_strip[blk])
            tokE[k] = S.emit("scalar", lambda e, g=g, gi=gi, par=par, sl=k % 2, s=s:
                             e.activation(out=ab_b[sl][:], in_=st_dl[par * 2 + gi][:],
                                          func=AF.Exp,
                                          scale=a_scale[:, g * 16 + s:g * 16 + s + 1]),
                             inc="cS")

        def p2_bx(k):
            blk, s, gi = k // 32, (k % 32) // 2, k % 2
            S.waitt("vector", tok_bc[(blk, s)])
            S.waitt("vector", tok_u[(blk, gi)])
            tokB[k] = S.emit("vector", lambda e, gi=gi, sl=k % 2, bs=s % 2: e.tensor_tensor(
                out=bx_b[sl][:], in0=u_b[gi][:], in1=bbc_b[bs][:], op=OP.mult), inc="cV")

        def p2_scan(k):
            S.waitt("vector", tokE[k])
            S.waitt("vector", tokB[k])
            S.wait("vector", "uYM", k - 1)
            tokSc[k] = S.emit("vector", lambda e, sl=k % 2, hl=k % 2: e.tensor_tensor_scan(
                out=hs_b[hl][:], data0=ab_b[sl][:], data1=bx_b[sl][:],
                initial=0.0, op0=OP.mult, op1=OP.add), inc="uAB")

        def p2_ym(k):
            blk, s = k // 32, (k % 32) // 2
            S.waitt("vector", tokSc[k])
            S.waitt("vector", tok_bc[(blk, s)])
            S.wait("vector", "uP", k - 1)
            tokY[k] = S.emit("vector", lambda e, hl=k % 2, bs=s % 2, pl=k % 2:
                             e.tensor_tensor(out=p_b[pl][:], in0=hs_b[hl][:],
                                             in1=cbc_b[bs][:], op=OP.mult), inc="uYM")

        def p2_mm(k):
            blk, s, gi = k // 32, (k % 32) // 2, k % 2
            S.waitt("tensor", tokY[k])
            if s == 0:
                S.wait("tensor", "uYF", 2 * blk)
            for q in range(4):
                S.emit("tensor", lambda e, gi=gi, pl=k % 2, q=q, s=s: e.matmul(
                    psY[gi][:, q * 512:(q + 1) * 512], id_f32r[:],
                    p_b[pl][:, q * 512:(q + 1) * 512],
                    start=(s == 0), stop=(s == DS - 1)),
                    inc="uP" if q == 3 else None)

        def p2_finalize(blk):
            par = blk % 2
            for gi in range(2):
                g = 2 * blk + gi
                S.wait("vector", "uP", 32 * blk + 31 + gi)
                t1 = S.emit("vector", lambda e, g=g, gi=gi, par=par: e.scalar_tensor_tensor(
                    out=u_b[gi][:], in0=st_xc[par * 2 + gi][:], scalar=d_par[:, g:g + 1],
                    in1=psY[gi][:], op0=OP.mult, op1=OP.add), inc="cV")
                S.waitt("vector", tok_sz[blk])
                S.waitt("vector", tok_ybf[0])
                S.waitt("vector", t1)
                t2 = S.emit("vector", lambda e, gi=gi: e.tensor_tensor(
                    out=yb_b[:], in0=u_b[gi][:], in1=sz_b[gi][:], op=OP.mult), inc="uYF")
                S.waitt("sync", t2)
                tok_ybf[0] = S.dma("sync", s_y[:, g, :], yb_b[:], "dS")

        w_out_bf = A(768, 8192).bitcast(BF16).rearrange("p (g t) -> p g t", g=NG)
        w_r_bf = A(8960, 4096).bitcast(BF16).rearrange("p (g t) -> p g t", g=GC)
        w_k_bf = A(13056, 4096).bitcast(BF16).rearrange("p (g t) -> p g t", g=GC)
        w_v_bf = A(17152, 4096).bitcast(BF16).rearrange("p (g t) -> p g t", g=GC)
        ytile = A(21248, 4096).bitcast(BF16).rearrange("p (g t) -> p g t", g=NG)
        xn_win = A(25344, 4104).rearrange("p (g t) -> p g t", g=GC)
        x_tok3 = A(29448, 2048).rearrange("p (g t) -> p g t", g=2)
        tmp3 = A(31496, 4096)
        xr_bf = A(35592, 2048).bitcast(BF16).rearrange("p (g t) -> p g t", g=GC)
        xk_bf = A(37640, 2048).bitcast(BF16).rearrange("p (g t) -> p g t", g=GC)
        r_sb = A(39688, 4096)
        kact_bf = A(43784, 2048).bitcast(BF16).rearrange("p (g t) -> p g t", g=GC)
        out_tok = A(39688, 4096).rearrange("p (g t) -> p g t", g=4)

        tok_ybf = [None]
        for k0 in range(NSTEP):
            blk, s, gi = k0 // 32, (k0 % 32) // 2, k0 % 2
            if k0 % 32 == 0:
                if blk >= 1:
                    p2_finalize(blk - 1)
                if blk == 0:
                    p2_strips(0)
                p2_u(blk)
            if gi == 0:
                p2_bc(blk, s)
            p2_exp(k0)
            p2_bx(k0)
            p2_scan(k0)
            p2_ym(k0)
            p2_mm(k0)
            if k0 % 32 == 8 and blk + 1 < NB:
                p2_strips(blk + 1)
            if k0 % 32 == 24:
                p2_sz(blk)
        p2_finalize(NB - 1)

        S.wait("sync", "uAB", 32 * NB)
        S.wait("sync", "uYM", 32 * NB)

        S.waitt("sync", tok_ybf[0])
        S.waitt("sync", bn_dma[0])
        S.waitt("sync", bn_dma[1])
        S.dma("sync", ytile[:, :, :], s_y[:, :, 0:512], "dS", mark=False)
        S.dma("sync", xn_win[:, :, 1:513], s_xn[:, :, 0:512], "dS", mark=False)
        tyl0 = S.marker("sync", "dS")
        S.waitt("sync", tok_strip[7])
        S.waitt("sync", tok_pc_st[0])
        S.waitt("sync", tok_pc_st[1])
        tok_wout = S.dma("sync", w_out_bf[:, :, :], s_wq[:, :, :], "dS")

        stB = [A(29448, 1024), A(30472, 1024)]
        tokB3 = [None, None]

        def p3_castB(wbf, wd, kcs, gate_yf):
            last = None
            for kc in kcs:
                sl = kc % 2
                S.waitt("sync", tokB3[sl])
                tst = S.dma("sync", stB[sl], wd[:, kc, :], "dS")
                S.waitt("scalar", tst)
                if gate_yf:
                    S.wait("scalar", "uYF", 2 * NB)
                tokB3[sl] = last = S.emit("scalar", lambda e, wbf=wbf, kc=kc, sl=sl: e.copy(
                    out=wbf[:, kc, :], in_=stB[sl][:]), inc="cS")
            return last

        tok_wr = p3_castB(w_r_bf, w_r_d, range(GC), False)
        tok_wk = p3_castB(w_k_bf, w_k_d, range(GC), True)
        tok_wv = p3_castB(w_v_bf, w_v_d, range(GC), True)

        tmz = S.emit("vector", lambda e: e.memset(xn_win[:, :, 0:1], 0.0), inc="cV")
        S.waitt("tensor", tok_wout)
        S.wait("tensor", "uYF", 2 * NB)

        Tc3 = 512
        relu_buf = A(21248, 4096)
        out_tok3 = A(31496, 4096).rearrange("p (g t) -> p g t", g=4)
        psq = [psA[:, 0:1024], psA[:, 1024:2048], psB[:, 0:1024], psB[:, 1024:2048]]
        out_wv = [A(35592, 2048), A(37640, 2048)]
        tok_ow = [None, None]
        tok_tr_prev = None
        tok_chunk_c = None
        tok_f1a_prev = None
        tok_mix_done = None
        xslot_free = [None, None]
        tyl_next = txn_next = tyl0
        for ch in range(4):
            t0 = ch * Tc3
            tyl = txn = tyl_next

            S.waitt("tensor", tyl)
            tom_q = {}
            for half, pp in ((0, psA), (1, psB)):
                if ch > 0:
                    if half == 0:
                        S.waitt("tensor", tok_f1a_prev)
                    else:
                        S.waitt("tensor", tok_chunk_c)
                for mc in range(4):
                    m = half * 4 + mc
                    for kc in range(NG):
                        tom = S.emit("tensor", lambda e, m=m, mc=mc, kc=kc, pp=pp: e.matmul(
                            pp[:, mc * 512:(mc + 1) * 512],
                            w_out_bf[:, kc, m * 128:m * 128 + 128],
                            ytile[:, kc, :], start=(kc == 0), stop=(kc == NG - 1)),
                            inc="cT" if kc == NG - 1 else None)
                        if kc == NG - 1:
                            tom_q[(half, mc)] = tom
            S.waitt("scalar", tok_tr_prev)
            tev_q = {}
            for half, pp in ((0, psA), (1, psB)):
                for mc in range(4):
                    S.waitt("scalar", tom_q[(half, mc)])
                    tev_q[(half, mc)] = S.emit(
                        "scalar", lambda e, half=half, mc=mc, pp=pp: e.copy(
                            out=tmp3[:, half * 2048 + mc * 512:half * 2048 + mc * 512 + 512],
                            in_=pp[:, mc * 512:(mc + 1) * 512]), inc="cS")
            tev_a = tev_q[(0, 3)]
            tev_b = tev_q[(1, 3)]

            S.waitt("vector", txn)
            S.waitt("vector", tmz)
            S.waitt("vector", tok_chunk_c)
            S.waitt("vector", tok_ow[0])
            S.waitt("vector", tok_ow[1])
            tms = S.emit("vector", lambda e: e.tensor_tensor(
                out=r_sb[:].rearrange("p (g t) -> p g t", g=GC)[:, :, :],
                in0=xn_win[:, :, 1:513], in1=xn_win[:, :, 0:512], op=OP.subtract), inc="cV")
            S.waitt("vector", tms)
            tmr = None
            for gc in range(GC):
                tmr = S.emit("vector", lambda e, gc=gc: e.scalar_tensor_tensor(
                    out=xr_bf[:, gc, :], in0=r_sb[:, gc * 512:(gc + 1) * 512],
                    scalar=mu_r[:, gc:gc + 1], in1=xn_win[:, gc, 0:512],
                    op0=OP.mult, op1=OP.add), inc="cV" if gc == GC - 1 else None)
            tmixl = None
            for gc in range(GC):
                tmixl = S.emit("vector", lambda e, gc=gc: e.scalar_tensor_tensor(
                    out=xk_bf[:, gc, :], in0=r_sb[:, gc * 512:(gc + 1) * 512],
                    scalar=mu_k[:, gc:gc + 1], in1=xn_win[:, gc, 0:512],
                    op0=OP.mult, op1=OP.add), inc="cV" if gc == GC - 1 else None)
            tok_mix_done = tmixl

            sig_q = {}
            for half, pp in ((0, psA), (1, psB)):
                S.waitt("tensor", tok_wr)
                S.waitt("tensor", tmr)
                for mc in range(4):
                    m = half * 4 + mc
                    S.waitt("tensor", tev_q[(half, mc)])
                    trm = None
                    for kc in range(GC):
                        trm = S.emit("tensor", lambda e, m=m, mc=mc, kc=kc, pp=pp: e.matmul(
                            pp[:, mc * 512:(mc + 1) * 512],
                            w_r_bf[:, kc, m * 128:m * 128 + 128],
                            xr_bf[:, kc, :], start=(kc == 0), stop=(kc == GC - 1)),
                            inc="cT" if kc == GC - 1 else None)
                    S.waitt("scalar", trm)
                    S.waitt("scalar", tmixl)
                    sig_q[(half, mc)] = S.emit(
                        "scalar", lambda e, half=half, mc=mc, pp=pp: e.activation(
                            out=r_sb[:, half * 2048 + mc * 512:half * 2048 + mc * 512 + 512],
                            in_=pp[:, mc * 512:(mc + 1) * 512], func=AF.Sigmoid), inc="cS")
            tok_sig = [sig_q[(0, 3)], sig_q[(1, 3)]]
            kq = {}
            for half, pp in ((0, psA), (1, psB)):
                S.waitt("tensor", tok_wk)
                S.waitt("tensor", tmixl)
                for mc in range(4):
                    m = half * 4 + mc
                    S.waitt("tensor", sig_q[(half, mc)])
                    tkm = None
                    for kc in range(GC):
                        tkm = S.emit("tensor", lambda e, m=m, mc=mc, kc=kc, pp=pp: e.matmul(
                            pp[:, mc * 512:(mc + 1) * 512],
                            w_k_bf[:, kc, m * 128:m * 128 + 128],
                            xk_bf[:, kc, :], start=(kc == 0), stop=(kc == GC - 1)),
                            inc="cT" if kc == GC - 1 else None)
                    S.waitt("scalar", tkm)
                    trl = S.emit("scalar", lambda e, half=half, mc=mc, pp=pp: e.activation(
                        out=relu_buf[:, half * 2048 + mc * 512:half * 2048 + mc * 512 + 512],
                        in_=pp[:, mc * 512:(mc + 1) * 512], func=AF.Relu), inc="cS")
                    S.waitt("scalar", trl)
                    kq[(half, mc)] = S.emit(
                        "scalar", lambda e, half=half, mc=mc: e.activation(
                            out=kact_bf.rearrange("p g t -> p (g t)")[
                                :, half * 2048 + mc * 512:half * 2048 + mc * 512 + 512],
                            in_=relu_buf[:, half * 2048 + mc * 512:half * 2048 + mc * 512 + 512],
                            func=AF.Square), inc="cS")
            tok_k = [kq[(0, 3)], kq[(1, 3)]]
            if ch < 3:
                t0n = (ch + 1) * Tc3
                S.waitt("sync", tok_k[0])
                S.waitt("sync", tok_k[1])
                S.waitt("sync", tok_ybf[0])
                S.dma("sync", ytile[:, :, :], s_y[:, :, t0n:t0n + Tc3], "dS", mark=False)
                S.waitt("sync", tok_mix_done)
                S.dma("sync", xn_win[:, :, :], s_xn[:, :, t0n - 1:t0n + Tc3], "dS",
                      mark=False)
                tyl_next = txn_next = S.marker("sync", "dS")
            tvm_q = {}
            for half, pp in ((0, psA), (1, psB)):
                S.waitt("tensor", tok_wv)
                S.waitt("tensor", tok_k[0])
                S.waitt("tensor", tok_k[1])
                for mc in range(4):
                    m = half * 4 + mc
                    for kc in range(GC):
                        tvm = S.emit("tensor", lambda e, m=m, mc=mc, kc=kc, pp=pp: e.matmul(
                            pp[:, mc * 512:(mc + 1) * 512],
                            w_v_bf[:, kc, m * 128:m * 128 + 128],
                            kact_bf[:, kc, :], start=(kc == 0), stop=(kc == GC - 1)),
                            inc="cT" if kc == GC - 1 else None)
                        if kc == GC - 1:
                            tvm_q[(half, mc)] = tvm
            tf1a = tf1b = None
            for half, pp in ((0, psA), (1, psB)):
                for mc in range(4):
                    S.waitt("vector", tvm_q[(half, mc)])
                    S.waitt("vector", sig_q[(half, mc)])
                    tq = S.emit("vector", lambda e, half=half, mc=mc, pp=pp: e.tensor_tensor(
                        out=r_sb[:, half * 2048 + mc * 512:half * 2048 + mc * 512 + 512],
                        in0=r_sb[:, half * 2048 + mc * 512:half * 2048 + mc * 512 + 512],
                        in1=pp[:, mc * 512:(mc + 1) * 512], op=OP.mult), inc="cV")
                    if half == 0:
                        tf1a = tq
                    else:
                        tf1b = tq
            S.waitt("vector", tf1a)
            S.waitt("vector", tev_a)
            tf2a = S.emit("vector", lambda e: e.tensor_tensor(
                out=tmp3[:, 0:2048], in0=tmp3[:, 0:2048], in1=r_sb[:, 0:2048], op=OP.add),
                inc="cV")
            S.waitt("vector", tf1b)
            S.waitt("vector", tev_b)
            tf2b = S.emit("vector", lambda e: e.tensor_tensor(
                out=tmp3[:, 2048:4096], in0=tmp3[:, 2048:4096], in1=r_sb[:, 2048:4096],
                op=OP.add), inc="cV")
            t_add = []
            for wave in range(2):
                S.waitt("tensor", tf2a)
                S.waitt("tensor", tf2b)
                S.waitt("tensor", tf1b)
                if wave == 1:
                    S.waitt("tensor", t_add[1])
                ttr = None
                for ti in range(2):
                    tsub = wave * 2 + ti
                    for gc in range(GC):
                        ttr = S.emit("tensor", lambda e, tsub=tsub, ti=ti, gc=gc: e.transpose(
                            psB[:, ti * 1024 + gc * 128:ti * 1024 + (gc + 1) * 128],
                            tmp3.rearrange("p (g t) -> p g t", g=GC)[:, gc,
                                                                    tsub * 128:(tsub + 1) * 128],
                            id_f32[:]),
                            inc="cT" if gc == GC - 1 else None)
                    for _ in (0,):
                        pass
                for ti in range(2):
                    tsub = wave * 2 + ti
                    sl = tsub % 2
                    if ch == 0 and tsub < 2:
                        S.waitt("sync", tok_wv)
                    S.waitt("sync", xslot_free[sl])
                    txx = S.dma("sync", x_tok3[:, sl, :],
                                x_d.rearrange("(n p) c -> p n c", p=P)[:, ch * 4 + tsub, :],
                                "dS")
                    S.waitt("vector", ttr)
                    S.waitt("vector", txx)
                    t_add.append(S.emit("vector", lambda e, wave=wave, sl=sl, ti=ti:
                                        e.tensor_tensor(
                        out=out_wv[wave][:, ti * 1024:ti * 1024 + 1024],
                        in0=x_tok3[:, sl, :],
                        in1=psB[:, ti * 1024:ti * 1024 + 1024], op=OP.add), inc="cV"))
                    xslot_free[sl] = t_add[-1]
                S.waitt("sync", t_add[-1])
                tok_ow[wave] = S.dma(
                    "sync",
                    out_d.rearrange("(n p) c -> p n c", p=P)[:,
                                                             ch * 4 + wave * 2:
                                                             ch * 4 + wave * 2 + 2, :],
                    out_wv[wave][:].rearrange("p (n c) -> p n c", n=2), "dS")
            tok_chunk_c = t_add[-1]
            tok_f1a_prev = tf1a
            tok_tr_prev = ttr

        S.wait("sync", "dS", S.cnt["dS"])
        S.wait("sync", "dA", S.cnt["dA"])

        def replay(eng_obj, name):
            for it in S.items[name]:
                if it[0] == "w":
                    eng_obj.wait_ge(sems[it[1]], it[2])
                elif it[0] == "i":
                    eng_obj.sem_inc(sems[it[1]], 1)
                else:
                    _, fn, inc, amt = it
                    inst = fn(eng_obj)
                    if inc is not None:
                        inst.then_inc(sems[inc], amt)

        @block.sync
        def _(eng):
            replay(eng, "sync")

        @block.tensor
        def _(eng):
            replay(eng, "tensor")

        @block.vector
        def _(eng):
            replay(eng, "vector")

        @block.scalar
        def _(eng):
            replay(eng, "scalar")

        @block.gpsimd
        def _(eng):
            replay(eng, "gpsimd")

    return nc


def _relayout(w, ng):
    return np.ascontiguousarray(np.asarray(w, np.float32).reshape(ng, P, -1).transpose(1, 0, 2))


def kernel(L0_attn_x, L0_attn_W_in, L0_attn_conv_w, L0_attn_conv_b, L0_attn_W_xproj,
           L0_attn_W_dt, L0_attn_b_dt, L0_attn_A_log, L0_attn_D_param, L0_attn_W_out,
           L0_ffn_W_r, L0_ffn_W_k, L0_ffn_W_v, L0_ffn_mu_r, L0_ffn_mu_k):
    x = np.ascontiguousarray(np.asarray(L0_attn_x, dtype=np.float32))
    ident = np.eye(P, dtype=np.float32)
    shared = {
        "w_in": _relayout(L0_attn_W_in, GC),
        "w_xp": _relayout(L0_attn_W_xproj, NG),
        "w_dt": np.ascontiguousarray(np.asarray(L0_attn_W_dt, np.float32)),
        "w_out": _relayout(L0_attn_W_out, NG),
        "w_r": _relayout(np.ascontiguousarray(np.asarray(L0_ffn_W_r, np.float32).T), GC),
        "w_k": _relayout(np.ascontiguousarray(np.asarray(L0_ffn_W_k, np.float32).T), GC),
        "w_v": _relayout(np.ascontiguousarray(np.asarray(L0_ffn_W_v, np.float32).T), GC),
        "a_log": _relayout(L0_attn_A_log, NG),
        "conv_w": _relayout(L0_attn_conv_w, NG),
        "conv_b": np.ascontiguousarray(np.asarray(L0_attn_conv_b, np.float32).reshape(NG, P).T),
        "b_dt": np.ascontiguousarray(np.asarray(L0_attn_b_dt, np.float32).reshape(NG, P).T),
        "d_par": np.ascontiguousarray(np.asarray(L0_attn_D_param, np.float32).reshape(NG, P).T),
        "mu_r": np.ascontiguousarray(np.asarray(L0_ffn_mu_r, np.float32).reshape(GC, P).T),
        "mu_k": np.ascontiguousarray(np.asarray(L0_ffn_mu_k, np.float32).reshape(GC, P).T),
        "ident": ident,
        "identr": ident,
    }
    nc = build_nc()
    in_maps = [dict(shared, x=x[b]) for b in range(8)]
    res = run_bass_kernel_spmd(nc, in_maps, core_ids=list(range(8)))
    out = np.stack([np.asarray(res.results[b]["out"]) for b in range(8)], axis=0)
    return out.astype(np.float32)


# version 41
# speedup vs baseline: 1.0189x; 1.0100x over previous
import contextlib
import numpy as np
import concourse.bass as bass
import concourse.mybir as mybir
from concourse.bass_utils import run_bass_kernel_spmd

F32 = mybir.dt.float32
F32R = mybir.dt.float32r
BF16 = mybir.dt.bfloat16
AF = mybir.ActivationFunctionType
OP = mybir.AluOpType
AX = mybir.AxisListType

P = 128
T = 2048
DM = 1024
DI = 2048
DS = 16
DTR = 64
NG = DI // P
GC = DM // P
NT = T // P
EPS = 1e-5
ARENA = 46848
ARENA_F32R = 4224

ENGINES = ("sync", "tensor", "vector", "scalar", "gpsimd")
SEMS = ("dS", "dA", "dG", "cV", "cS", "cT", "cG", "uAB", "uYM", "uP", "uYF", "dM", "dN", "dH")


class Sched:

    def __init__(self):
        self.items = {e: [] for e in ENGINES}
        self.cnt = {s: 0 for s in SEMS}
        self.mark = {e: {s: 0 for s in SEMS} for e in ENGINES}

    def wait(self, eng, sem, val):
        if val <= 0 or self.mark[eng][sem] >= val:
            return
        self.mark[eng][sem] = val
        self.items[eng].append(("w", sem, val))

    def waitt(self, eng, tok):
        if tok is not None:
            self.wait(eng, tok[0], tok[1])

    def emit(self, eng, fn, inc=None, amt=1):
        tok = None
        if inc is not None:
            self.cnt[inc] += amt
            tok = (inc, self.cnt[inc])
        self.items[eng].append(("e", fn, inc, amt))
        return tok

    def dma(self, q, out, in_, sem, mark=True):
        self.emit(q, lambda e, o=out, i=in_: e.dma_start(out=o, in_=i), inc=sem, amt=16)
        if mark:
            return self.marker(q, sem)
        return None

    def marker(self, q, dsem):
        msem = {"dS": "dM", "dG": "dH"}.get(dsem, "dN")
        self.wait(q, dsem, self.cnt[dsem])
        self.cnt[msem] += 1
        tok = (msem, self.cnt[msem])
        self.items[q].append(("i", msem))
        return tok


def build_nc():
    nc = bass.Bass("TRN2", target_bir_lowering=False)

    x_d = nc.dram_tensor("x", [T, DM], F32, kind="ExternalInput").ap()
    w_in_d = nc.dram_tensor("w_in", [P, GC, 2 * DI], F32, kind="ExternalInput").ap()
    w_xp_d = nc.dram_tensor("w_xp", [P, NG, 96], F32, kind="ExternalInput").ap()
    w_dt_d = nc.dram_tensor("w_dt", [DTR, DI], F32, kind="ExternalInput").ap()
    w_out_d = nc.dram_tensor("w_out", [P, NG, DM], F32, kind="ExternalInput").ap()
    w_r_d = nc.dram_tensor("w_r", [P, GC, DM], F32, kind="ExternalInput").ap()
    w_k_d = nc.dram_tensor("w_k", [P, GC, DM], F32, kind="ExternalInput").ap()
    w_v_d = nc.dram_tensor("w_v", [P, GC, DM], F32, kind="ExternalInput").ap()
    alog_d = nc.dram_tensor("a_log", [P, NG, DS], F32, kind="ExternalInput").ap()
    cw_d = nc.dram_tensor("conv_w", [P, NG, 4], F32, kind="ExternalInput").ap()
    cb_d = nc.dram_tensor("conv_b", [P, NG], F32, kind="ExternalInput").ap()
    bdt_d = nc.dram_tensor("b_dt", [P, NG], F32, kind="ExternalInput").ap()
    dpar_d = nc.dram_tensor("d_par", [P, NG], F32, kind="ExternalInput").ap()
    mur_d = nc.dram_tensor("mu_r", [P, GC], F32, kind="ExternalInput").ap()
    muk_d = nc.dram_tensor("mu_k", [P, GC], F32, kind="ExternalInput").ap()
    idr_d = nc.dram_tensor("identr", [P, P], F32R, kind="ExternalInput").ap()
    id_d = nc.dram_tensor("ident", [P, P], F32, kind="ExternalInput").ap()
    out_d = nc.dram_tensor("out", [T, DM], F32, kind="ExternalOutput").ap()

    s_xb = nc.dram_tensor("s_xb", [P, NG, T], F32, kind="Internal").ap()
    s_xc = nc.dram_tensor("s_xc", [P, NG, T], F32, kind="Internal").ap()
    s_dl = nc.dram_tensor("s_dl", [P, NG, T], F32, kind="Internal").ap()
    s_sz = nc.dram_tensor("s_sz", [P, NG, T], F32, kind="Internal").ap()
    s_xn = nc.dram_tensor("s_xn", [P, GC, T], F32, kind="Internal").ap()
    s_bc = nc.dram_tensor("s_bc", [2 * DS, T], F32, kind="Internal").ap()
    s_y = nc.dram_tensor("s_y", [P, NG, T], BF16, kind="Internal").ap()
    s_wq = nc.dram_tensor("s_wq", [P, 16, 1024], BF16, kind="Internal").ap()

    with contextlib.ExitStack() as stk:
        E = stk.enter_context
        ar = E(nc.sbuf_tensor("arena", [P, ARENA], F32))
        arF = E(nc.sbuf_tensor("arenaF", [P, ARENA_F32R], F32R))
        psA = E(nc.psum_tensor("psA", [P, 2048], F32))
        psB = E(nc.psum_tensor("psB", [P, 2048], F32))
        sems = {s: E(nc.semaphore(s)) for s in SEMS}
        block = E(nc.Block())

        S = Sched()

        def A(c0, n, rows=slice(0, P)):
            return ar[rows, c0:c0 + n]

        a_scale = A(0, 256)
        conv_w = A(256, 64)
        conv_b = A(320, 16)
        b_dt = A(336, 16)
        d_par = A(352, 16)
        mu_r = A(368, 8)
        mu_k = A(376, 8)
        id_f32 = A(384, 128)
        id_f32r = arF[:, 4096:4224]
        stt_sum = A(640, 16)
        stt_sq = A(656, 16)
        stt_m = A(672, 16)
        stt_v = A(688, 16)
        stt_rs = A(704, 16)
        stt_nb = A(720, 16)
        stt_t = A(736, 16)
        eps_ap = A(752, 1)
        neg1_ap = A(753, 1)

        S.emit("vector", lambda e: e.memset(eps_ap[:], EPS))
        tok_consts = S.emit("vector", lambda e: e.memset(neg1_ap[:], -1.0), inc="cV")
        tok_id = None
        tok_idr = None
        for dst, src in [(a_scale, alog_d), (conv_w, cw_d), (conv_b, cb_d),
                         (b_dt, bdt_d), (d_par, dpar_d), (mu_r, mur_d), (mu_k, muk_d)]:
            S.dma("sync", dst, src, "dS", mark=False)
        S.dma("sync", id_f32, id_d, "dS", mark=False)
        S.dma("sync", id_f32r, idr_d, "dS", mark=False)
        tok_par_last = tok_id = tok_idr = S.marker("sync", "dS")
        w_xp = A(43784, 768).bitcast(BF16).rearrange("p (g m) -> p g m", g=NG)
        w_dt = A(44552, 1024, slice(0, DTR)).bitcast(BF16)
        wxp_stage = A(35592, 1536)
        wdt_stage = A(37640, 2048, slice(0, DTR))
        t_ = S.dma("sync", wxp_stage, w_xp_d.rearrange("p g m -> p (g m)"), "dS")
        S.waitt("scalar", t_)
        tok_wxp = S.emit("scalar", lambda e: e.copy(
            out=A(43784, 768).bitcast(BF16), in_=wxp_stage), inc="cS")
        t_ = S.dma("sync", wdt_stage, w_dt_d, "dS")
        S.waitt("scalar", t_)
        tok_wdt = S.emit("scalar", lambda e: e.copy(
            out=A(44552, 1024, slice(0, DTR)).bitcast(BF16),
            in_=wdt_stage), inc="cS")

        x_full = A(768, NT * 1024)
        xn_bf_full = A(17152, 8192).bitcast(BF16)
        junk = A(25344, 1024)
        xn_tok = [A(26368, 1024), A(27392, 1024)]
        xn_bnc = [A(28416, 1024), A(29440, 1024)]
        ps_half = [psA[:, 0:1024], psA[:, 1024:2048]]

        tok_x = []
        for i in range(NT):
            S.dma("sync", x_full[:, i * 1024:(i + 1) * 1024],
                  x_d.rearrange("(n p) c -> n p c", p=P)[i], "dS", mark=False)
            if i % 4 == 3:
                mk = S.marker("sync", "dS")
                tok_x.extend([mk] * 4)
        last_vstat = None
        last_sstat = None
        for i in range(NT):
            S.waitt("vector", tok_x[i])
            last_vstat = S.emit("vector", lambda e, i=i: e.tensor_reduce(
                out=stt_sum[:, i:i + 1], in_=x_full[:, i * 1024:(i + 1) * 1024],
                axis=AX.X, op=OP.add), inc="cV")
            S.waitt("scalar", tok_x[i])
            S.waitt("scalar", last_sstat)
            last_sstat = S.emit("scalar", lambda e, i=i: e.activation(
                out=junk[:], in_=x_full[:, i * 1024:(i + 1) * 1024], func=AF.Square,
                accum_out=stt_sq[:, i:i + 1]), inc="cS")
        S.waitt("vector", last_sstat)
        S.waitt("vector", last_vstat)
        tk1 = S.emit("vector", lambda e: e.tensor_scalar(out=stt_m[:], in0=stt_sum[:],
                                                         scalar1=1.0 / DM, scalar2=None,
                                                         op0=OP.mult), inc="cV")
        tk2 = S.emit("vector", lambda e: e.tensor_scalar(out=stt_v[:], in0=stt_sq[:],
                                                         scalar1=1.0 / DM, scalar2=None,
                                                         op0=OP.mult), inc="cV")
        S.waitt("vector", tk1)
        tk3 = S.emit("vector", lambda e: e.tensor_tensor(out=stt_t[:], in0=stt_m[:],
                                                         in1=stt_m[:], op=OP.mult), inc="cV")
        S.waitt("vector", tk2)
        S.waitt("vector", tk3)
        tv = S.emit("vector", lambda e: e.tensor_tensor(out=stt_v[:], in0=stt_v[:],
                                                        in1=stt_t[:], op=OP.subtract), inc="cV")
        S.waitt("scalar", tv)
        S.waitt("scalar", tok_consts)
        tsq = S.emit("scalar", lambda e: e.activation(out=stt_t[:], in_=stt_v[:],
                                                      func=AF.Sqrt, bias=eps_ap[:]), inc="cS")
        S.waitt("vector", tsq)
        tk4 = S.emit("vector", lambda e: e.reciprocal(out=stt_rs[:], in_=stt_t[:]), inc="cV")
        S.waitt("vector", tk4)
        tk5 = S.emit("vector", lambda e: e.tensor_tensor(out=stt_nb[:], in0=stt_m[:],
                                                         in1=stt_rs[:], op=OP.mult), inc="cV")
        S.waitt("vector", tk5)
        tnb = S.emit("vector", lambda e: e.tensor_scalar(out=stt_nb[:], in0=stt_nb[:],
                                                         scalar1=-1.0, scalar2=None,
                                                         op0=OP.mult), inc="cV")

        w_in_bf = A(768, 16384).bitcast(BF16).rearrange("p (g t) -> p g t", g=GC)
        w_stage = [A(o, 2048) for o in (35592, 37640, 39688, 41736, 30464, 32512)]
        NSL = 6
        tok_stage = [None] * NSL
        tok_cast = [None] * NSL

        def stage_load(u):
            half, kc = u % 2, u // 2
            sl = u % NSL
            if u < 2:
                S.waitt("gpsimd", tok_wxp)
                S.waitt("gpsimd", tok_wdt)
            S.waitt("gpsimd", tok_cast[sl])
            tok_stage[sl] = S.dma("gpsimd", w_stage[sl],
                                  w_in_d[:, kc, half * 2048:(half + 1) * 2048], "dG")

        def stage_cast(u):
            half, kc = u % 2, u // 2
            sl = u % NSL
            S.waitt("scalar", tok_stage[sl])
            tok_cast[sl] = S.emit("scalar", lambda e, kc=kc, half=half, sl=sl: e.copy(
                out=w_in_bf[:, kc, half * 2048:(half + 1) * 2048],
                in_=w_stage[sl]), inc="cS")

        for u_ in range(NSL):
            stage_load(u_)

        tr_tok = [None, None]
        ev_tok = [None, None]
        cg_tok = [None, None]
        bn_dma = [None, None]
        tok_xn_w = []
        ta = None
        for i in range(NT):
            sl = i % 2
            S.waitt("vector", tnb)
            S.waitt("vector", tr_tok[sl])
            tp1 = S.emit("vector", lambda e, i=i, sl=sl: e.tensor_scalar(
                out=xn_tok[sl][:], in0=x_full[:, i * 1024:(i + 1) * 1024],
                scalar1=stt_rs[:, i:i + 1], scalar2=None, op0=OP.mult), inc="cV")
            S.waitt("vector", tp1)
            ta = S.emit("vector", lambda e, i=i, sl=sl: e.tensor_scalar(
                out=xn_tok[sl][:], in0=xn_tok[sl][:],
                scalar1=stt_nb[:, i:i + 1], scalar2=None, op0=OP.add), inc="cV")
            S.waitt("tensor", ta)
            S.waitt("tensor", tok_id)
            S.waitt("tensor", ev_tok[sl])
            S.waitt("tensor", cg_tok[sl])
            tt = None
            for gc in range(GC):
                tt = S.emit("tensor", lambda e, gc=gc, sl=sl: e.transpose(
                    ps_half[sl][:, gc * 128:(gc + 1) * 128],
                    xn_tok[sl][:, gc * 128:(gc + 1) * 128], id_f32[:]),
                    inc="cT" if gc == GC - 1 else None)
            tr_tok[sl] = tt
            S.waitt("scalar", tt)
            S.waitt("scalar", bn_dma[sl])
            S.waitt("scalar", cg_tok[sl])
            ev_tok[sl] = S.emit("scalar", lambda e, sl=sl: e.copy(
                out=xn_bnc[sl][:], in_=ps_half[sl]), inc="cS")
            cg_tok[sl] = S.emit("scalar", lambda e, i=i, sl=sl: e.copy(
                out=xn_bf_full.rearrange("p (g t) -> p g t", g=GC)[:, :, i * 128:(i + 1) * 128],
                in_=ps_half[sl][:].rearrange("p (g t) -> p g t", g=GC)), inc="cS")
            S.waitt("sync", ev_tok[sl])
            bn_dma[sl] = S.dma("sync", s_xn[:, :, i * 128:(i + 1) * 128],
                               xn_bnc[sl][:].rearrange("p (g t) -> p g t", g=GC), "dS")
            tok_xn_w.append(bn_dma[sl])
            S.waitt("scalar", ta)
            stage_cast(i)
            if i + NSL < 2 * GC:
                stage_load(i + NSL)
        tok_cast_h = [tok_cast[2], tok_cast[3]]

        xb_win = [A(25344, 2052), A(27396, 2052)]
        xc_b = [A(29448, 2048), A(31496, 2048)]
        xcbf_b = [A(33544, 1024).bitcast(BF16), A(34568, 1024).bitcast(BF16)]

        S.waitt("vector", tr_tok[0])
        S.waitt("vector", tr_tok[1])
        S.waitt("vector", last_sstat)
        S.emit("vector", lambda e: e.memset(xb_win[0][:, 0:3], 0.0))
        tvz = S.emit("vector", lambda e: e.memset(xb_win[1][:, 0:3], 0.0), inc="cV")

        tok_xc_w = {}
        tok_xcr = [None, None]
        tok_xc_dma = [None, None]
        tok_projslot = [None, None]
        tok_win_free = [None, None]
        tok_tk = {}
        tok_xcrg = {}
        tok_proj = None

        def do_conv(g):
            sl = g % 2
            if g == 0:
                S.waitt("vector", tok_cast[4])
                S.waitt("vector", bn_dma[0])
                S.waitt("vector", bn_dma[1])
            if g == 1:
                S.waitt("vector", tok_cast[4])
                S.waitt("vector", tok_cast[5])
            S.waitt("vector", tok_xb_w[(g, 0)])
            S.waitt("vector", tok_xb_w[(g, 1)])
            S.waitt("vector", tok_xcr[sl])
            S.waitt("vector", tok_xc_dma[sl])
            t0 = S.emit("vector", lambda e, g=g, sl=sl: e.tensor_scalar(
                out=xc_b[sl][:], in0=xb_win[sl][:, 0:2048],
                scalar1=conv_w[:, 4 * g:4 * g + 1], scalar2=None,
                op0=OP.mult), inc="cV")
            tk = t0
            for k in (1, 2, 3):
                S.waitt("vector", tk)
                tk = S.emit("vector", lambda e, g=g, sl=sl, k=k: e.scalar_tensor_tensor(
                    out=xc_b[sl][:], in0=xb_win[sl][:, k:k + 2048],
                    scalar=conv_w[:, 4 * g + k:4 * g + k + 1], in1=xc_b[sl][:],
                    op0=OP.mult, op1=OP.add), inc="cV")
            tok_win_free[sl] = tk
            tok_tk[g] = tk

        def do_finish(g):
            sl = g % 2
            S.waitt("scalar", tok_tk[g])
            tsl = S.emit("scalar", lambda e, g=g, sl=sl: e.activation(
                out=xc_b[sl][:], in_=xc_b[sl][:], func=AF.Silu,
                bias=conv_b[:, g:g + 1]), inc="cS")
            S.waitt("vector", tsl)
            S.waitt("vector", tok_projslot[sl])
            if g == 0:
                S.waitt("vector", tok_cast[5])
            tcr = S.emit("vector", lambda e, sl=sl: e.tensor_scalar(
                out=xcbf_b[sl][:], in0=xc_b[sl][:], scalar1=1.0, scalar2=None,
                op0=OP.mult), inc="cV")
            tok_xcr[sl] = tcr
            tok_xcrg[g] = tcr
            S.waitt("sync", tsl)
            tok_xc_w[g] = S.dma("sync", s_xc[:, g, :], xc_b[sl][:], "dS")
            tok_xc_dma[sl] = tok_xc_w[g]

        def do_proj(g):
            nonlocal tok_proj
            sl = g % 2
            S.waitt("tensor", tok_xcrg[g])
            S.waitt("tensor", tok_wxp)
            if g == 0:
                S.waitt("tensor", ev_tok[0])
                S.waitt("tensor", ev_tok[1])
                S.waitt("tensor", cg_tok[0])
                S.waitt("tensor", cg_tok[1])
            for q in range(4):
                tok_proj = S.emit("tensor", lambda e, g=g, sl=sl, q=q: e.matmul(
                    psA[0:96, q * 512:(q + 1) * 512], w_xp[:, g, :],
                    xcbf_b[sl][:, q * 512:(q + 1) * 512],
                    start=(g == 0), stop=(g == NG - 1)),
                    inc="cT" if q == 3 else None)
            tok_projslot[sl] = tok_proj

        mm_bnc = [A(39688, 1024), A(41736, 1024)]
        xn_bf = xn_bf_full.rearrange("p (g t) -> p g t", g=GC)

        tok_evac = [None, None]
        tok_bnc_dma = [None, None]
        tok_xb_w = {}
        tok_sz_w = {}
        pc_src = [A(35592, 1024), A(37640, 1024)]
        pc_dst = [A(36616, 512).bitcast(BF16), A(38664, 512).bitcast(BF16)]
        tok_pc_ld = [None, None]
        tok_pc_cast = [None, None]
        tok_pc_st = [None, None]

        def pc_unit(u):
            sl = u % 2
            if u < 2:
                S.waitt("gpsimd", tok_cast[0])
                S.waitt("gpsimd", tok_cast[1])
            S.waitt("gpsimd", tok_pc_cast[sl])
            tok_pc_ld[sl] = S.dma("gpsimd", pc_src[sl][:], w_out_d[:, u, :], "dG")
            S.waitt("scalar", tok_pc_ld[sl])
            S.waitt("scalar", tok_pc_st[sl])
            tok_pc_cast[sl] = S.emit("scalar", lambda e, sl=sl: e.copy(
                out=pc_dst[sl][:], in_=pc_src[sl][:]), inc="cS")
            S.waitt("gpsimd", tok_pc_cast[sl])
            tok_pc_st[sl] = S.dma("gpsimd", s_wq[:, u, :], pc_dst[sl][:], "dG")

        S.waitt("tensor", cg_tok[0])
        S.waitt("tensor", cg_tok[1])
        for m in range(16):
            for half in range(2):
                for hh in range(2):
                    slu = (half * 2 + hh) % 2
                    S.waitt("tensor", tok_cast_h[half])
                    S.waitt("tensor", tok_evac[slu])
                    tmm = None
                    for q in range(2):
                        col = slu * 1024 + q * 512
                        tc0 = hh * 1024 + q * 512
                        for kc in range(GC):
                            tmm = S.emit("tensor", lambda e, kc=kc, m=m, half=half,
                                         col=col, tc0=tc0: e.matmul(
                                psB[:, col:col + 512],
                                w_in_bf[:, kc, half * 2048 + m * 128:half * 2048 + m * 128 + 128],
                                xn_bf[:, kc, tc0:tc0 + 512],
                                start=(kc == 0), stop=(kc == GC - 1)),
                                inc="cT" if (kc == GC - 1 and q == 1) else None)
                    S.waitt("scalar", tmm)
                    if half == 0:
                        sl = m % 2
                        if m == 1:
                            S.waitt("scalar", bn_dma[0])
                            S.waitt("scalar", bn_dma[1])
                        S.waitt("scalar", tvz)
                        S.waitt("scalar", tok_win_free[sl])
                        tok_evac[slu] = S.emit("scalar", lambda e, hh=hh, sl=sl, slu=slu: e.copy(
                            out=xb_win[sl][:, 3 + hh * 1024:3 + hh * 1024 + 1024],
                            in_=psB[:, slu * 1024:slu * 1024 + 1024]), inc="cS")
                        tok_xb_w[(m, hh)] = tok_evac[slu]
                    else:
                        S.waitt("scalar", tok_bnc_dma[hh])
                        tok_evac[slu] = S.emit("scalar", lambda e, hh=hh, slu=slu: e.activation(
                            out=mm_bnc[hh][:], in_=psB[:, slu * 1024:slu * 1024 + 1024],
                            func=AF.Silu), inc="cS")
                        S.waitt("sync", tok_evac[slu])
                        tok_bnc_dma[hh] = S.dma("sync",
                                                s_sz[:, m, hh * 1024:hh * 1024 + 1024],
                                                mm_bnc[hh][:], "dS")
                        tok_sz_w[(m, hh)] = tok_bnc_dma[hh]
            do_conv(m)
            if m >= 1:
                do_finish(m - 1)
            if m >= 2:
                do_proj(m - 2)
            pc_unit(m)
        do_finish(15)
        do_proj(14)
        do_proj(15)
        tok_sz_last = tok_bnc_dma[1]

        dr = A(45576, 1024, slice(0, DTR)).bitcast(BF16)
        dl_b = [A(35592, 2048), A(37640, 2048)]
        tmpA = A(39688, 2048)
        tmpB = A(41736, 2048)
        bc_bnc = A(41736, 2048, slice(DTR, 96))

        S.waitt("scalar", tok_par_last)
        S.waitt("scalar", tok_wxp)
        S.waitt("scalar", tok_wdt)
        S.waitt("scalar", tok_bnc_dma[0])
        S.waitt("scalar", tok_bnc_dma[1])
        tA1 = S.emit("scalar", lambda e: e.activation(out=tmpA[:, 0:256], in_=a_scale[:],
                                                      func=AF.Exp), inc="cS")
        S.waitt("vector", tA1)
        tA2 = S.emit("vector", lambda e: e.tensor_scalar(out=a_scale[:], in0=tmpA[:, 0:256],
                                                         scalar1=-1.0, scalar2=None,
                                                         op0=OP.mult), inc="cV")

        S.waitt("vector", tok_proj)
        tdr = S.emit("vector", lambda e: e.tensor_scalar(
            out=dr[:], in0=psA[0:DTR, :], scalar1=1.0, scalar2=None, op0=OP.mult), inc="cV")
        S.waitt("scalar", tok_proj)
        tbc = S.emit("scalar", lambda e: e.copy(out=bc_bnc[:], in_=psA[DTR:96, :]), inc="cS")
        S.waitt("sync", tbc)
        tok_bc_w = S.dma("sync", s_bc[:, :], bc_bnc[:], "dA")

        tok_dl_w = {}
        tok_dl_dma = [None, None]
        tok_sp_s = [None, None]
        tok_sp_v = [None, None]
        ps_d = [psA, psB]
        tmpA4 = [A(39688, 2048), A(30464, 2048), A(25344, 2048), A(27392, 2048)]
        tmpB2 = [A(41736, 2048), A(32512, 2048)]
        tok_t5_a = [None, None, None, None]
        tok_t5_b = [None, None]
        tok_dm = {}
        tok_t1 = {}
        tok_t2 = {}
        tok_t3 = {}
        tok_t4 = {}

        def d_mm(g):
            sl = g % 2
            pp = ps_d[sl]
            S.waitt("tensor", tdr)
            if g == 0:
                S.waitt("tensor", tok_bc_w)
            if g == 1:
                S.waitt("tensor", tok_evac[0])
                S.waitt("tensor", tok_evac[1])
            S.waitt("tensor", tok_sp_s[sl])
            S.waitt("tensor", tok_sp_v[sl])
            S.waitt("tensor", tok_wdt)
            tdm = None
            for q in range(4):
                tdm = S.emit("tensor", lambda e, g=g, pp=pp, q=q: e.matmul(
                    pp[:, q * 512:(q + 1) * 512], w_dt[:, g * 128:(g + 1) * 128],
                    dr[:, q * 512:(q + 1) * 512], start=True, stop=True),
                    inc="cT" if q == 3 else None)
            tok_dm[g] = tdm

        def d_s1(g):
            sl, s4 = g % 2, g % 4
            pp = ps_d[sl]
            S.waitt("scalar", tok_dm[g])
            if g == 1:
                S.waitt("scalar", tok_xc_dma[0])
                S.waitt("scalar", tok_xc_dma[1])
            if g == 2:
                S.waitt("scalar", tok_tk[15])
            S.waitt("scalar", tok_t5_a[s4])
            tok_t1[g] = S.emit("scalar", lambda e, g=g, pp=pp, s4=s4: e.activation(
                out=tmpA4[s4][:], in_=pp[:], func=AF.Abs, bias=b_dt[:, g:g + 1]),
                inc="cS")
            tok_sp_s[sl] = tok_t1[g]

        def d_s2(g):
            sl, s4 = g % 2, g % 4
            S.waitt("scalar", tok_t1[g])
            if g == 0:
                S.waitt("scalar", tok_bc_w)
            S.waitt("scalar", tok_t5_b[sl])
            tok_t2[g] = S.emit("scalar", lambda e, sl=sl, s4=s4: e.activation(
                out=tmpB2[sl][:], in_=tmpA4[s4][:], func=AF.Exp, scale=neg1_ap[:]),
                inc="cS")

        def d_s3(g):
            sl, s4 = g % 2, g % 4
            S.waitt("scalar", tok_t2[g])
            tok_t3[g] = S.emit("scalar", lambda e, sl=sl, s4=s4: e.activation(
                out=tmpA4[s4][:], in_=tmpB2[sl][:], func=AF.Ln, bias=1.0), inc="cS")

        def d_v4(g):
            sl = g % 2
            pp = ps_d[sl]
            S.waitt("vector", tok_dm[g])
            S.waitt("vector", tok_t3[g])
            tok_t4[g] = S.emit("vector", lambda e, g=g, pp=pp, sl=sl: e.tensor_scalar(
                out=tmpB2[sl][:], in0=pp[:], scalar1=b_dt[:, g:g + 1], scalar2=0.0,
                op0=OP.add, op1=OP.max), inc="cV")
            tok_sp_v[sl] = tok_t4[g]

        def d_v5(g):
            sl, s4 = g % 2, g % 4
            if g <= 1:
                S.waitt("vector", tok_pc_st[0])
                S.waitt("vector", tok_pc_st[1])
            S.waitt("vector", tok_t4[g])
            S.waitt("vector", tok_t3[g])
            S.waitt("vector", tok_dl_dma[sl])
            S.waitt("vector", tok_wxp)
            S.waitt("vector", tok_wdt)
            t5 = S.emit("vector", lambda e, sl=sl, s4=s4: e.tensor_tensor(
                out=dl_b[sl][:], in0=tmpA4[s4][:], in1=tmpB2[sl][:], op=OP.add),
                inc="cV")
            tok_t5_a[s4] = tok_t5_b[sl] = t5
            S.waitt("sync", t5)
            tdm = S.dma("sync", s_dl[:, g, :], dl_b[sl][:], "dS", mark=(g % 2 == 1))
            tok_dl_w[g] = tdm
            if tdm is not None:
                tok_dl_w[g - 1] = tdm
                tok_dl_dma[0] = tok_dl_dma[1] = tdm

        for gp in range(0, NG, 2):
            d_mm(gp)
            d_mm(gp + 1)
            d_s1(gp)
            d_s1(gp + 1)
            d_s2(gp)
            d_s2(gp + 1)
            d_s3(gp)
            d_s3(gp + 1)
            d_v4(gp)
            d_v4(gp + 1)
            d_v5(gp)
            d_v5(gp + 1)


        st_dl = [A(768 + 2048 * i, 2048) for i in range(4)]
        st_xc = [A(8960 + 2048 * i, 2048) for i in range(4)]
        u_b = [A(17152, 2048), A(19200, 2048)]
        ab_b = [A(21248, 2048), A(23296, 2048)]
        bx_b = [A(25344, 2048), A(27392, 2048)]
        hs_b = [A(29440, 2048), A(31488, 2048)]
        p_b = [arF[:, 0:2048], arF[:, 2048:4096]]
        bbc_b = [A(33536, 2048), A(35584, 2048)]
        cbc_b = [A(37632, 2048), A(39680, 2048)]
        sz_b = [A(41728, 2048), A(43776, 2048)]
        yb_b = A(45824, 1024).bitcast(BF16)
        psY = [psA, psB]

        NB = NG // 2
        NSTEP = 32 * NB

        S.waitt("tensor", tok_sp_s[0])
        S.waitt("tensor", tok_sp_s[1])
        S.waitt("tensor", tok_sp_v[0])
        S.waitt("tensor", tok_sp_v[1])
        S.waitt("tensor", tok_idr)
        S.waitt("scalar", tA2)
        for _sl4 in range(NSL):
            S.waitt("sync", tok_cast[_sl4])

        tok_strip = {}
        tok_u = {}
        tok_bc = {}
        tok_sz = {}
        tokE = {}
        tokB = {}
        tokSc = {}
        tokY = {}
        tok_ybf_dma = None

        def p2_strips(blk):
            par = blk % 2
            S.wait("sync", "uYF", 2 * (blk - 1))
            for gi in range(2):
                g = 2 * blk + gi
                S.waitt("sync", tok_dl_w[g])
                S.waitt("sync", tok_xc_w[g])
                S.dma("sync", st_dl[par * 2 + gi], s_dl[:, g, :], "dS", mark=False)
                S.dma("sync", st_xc[par * 2 + gi], s_xc[:, g, :], "dS", mark=False)
            if blk == 0:
                S.waitt("sync", tok_sz_last)
                S.waitt("sync", tok_bc_w)
            tok_strip[blk] = S.marker("sync", "dS")

        def p2_sz(blk):
            S.waitt("sync", tok_sz_last)
            S.wait("sync", "uYF", 2 * blk)
            for gi in range(2):
                S.dma("sync", sz_b[gi], s_sz[:, 2 * blk + gi, :], "dS", mark=False)
            tok_sz[blk] = S.marker("sync", "dS")

        def p2_u(blk):
            par = blk % 2
            for gi in range(2):
                S.waitt("vector", tok_strip[blk])
                S.wait("vector", "uAB", 32 * blk)
                S.wait("vector", "uYF", 2 * blk)
                tok_u[(blk, gi)] = S.emit("vector", lambda e, par=par, gi=gi:
                                          e.tensor_tensor(out=u_b[gi][:],
                                                          in0=st_dl[par * 2 + gi][:],
                                                          in1=st_xc[par * 2 + gi][:],
                                                          op=OP.mult), inc="cV")

        def p2_bc(blk, s):
            bs = s % 2
            S.waitt("gpsimd", tok_dl_w[NG - 1])
            S.waitt("gpsimd", tok_bc_w)
            base = 32 * blk
            need = base + 2 * (s - 2) + 2 if s >= 2 else base
            S.wait("gpsimd", "uAB", need)
            S.wait("gpsimd", "uYM", need)
            S.dma("gpsimd", bbc_b[bs], s_bc[s:s + 1, :].partition_broadcast(P), "dG",
                  mark=False)
            S.dma("gpsimd", cbc_b[bs], s_bc[DS + s:DS + s + 1, :].partition_broadcast(P),
                  "dG", mark=False)
            tok_bc[(blk, s)] = S.marker("gpsimd", "dG")

        def p2_exp(k):
            blk, s, gi = k // 32, (k % 32) // 2, k % 2
            g = 2 * blk + gi
            par = blk % 2
            S.wait("scalar", "uAB", k - 1)
            S.waitt("scalar", tok_strip[blk])
            tokE[k] = S.emit("scalar", lambda e, g=g, gi=gi, par=par, sl=k % 2, s=s:
                             e.activation(out=ab_b[sl][:], in_=st_dl[par * 2 + gi][:],
                                          func=AF.Exp,
                                          scale=a_scale[:, g * 16 + s:g * 16 + s + 1]),
                             inc="cS")

        def p2_bx(k):
            blk, s, gi = k // 32, (k % 32) // 2, k % 2
            S.waitt("vector", tok_bc[(blk, s)])
            S.waitt("vector", tok_u[(blk, gi)])
            tokB[k] = S.emit("vector", lambda e, gi=gi, sl=k % 2, bs=s % 2: e.tensor_tensor(
                out=bx_b[sl][:], in0=u_b[gi][:], in1=bbc_b[bs][:], op=OP.mult), inc="cV")

        def p2_scan(k):
            S.waitt("vector", tokE[k])
            S.waitt("vector", tokB[k])
            S.wait("vector", "uYM", k - 1)
            tokSc[k] = S.emit("vector", lambda e, sl=k % 2, hl=k % 2: e.tensor_tensor_scan(
                out=hs_b[hl][:], data0=ab_b[sl][:], data1=bx_b[sl][:],
                initial=0.0, op0=OP.mult, op1=OP.add), inc="uAB")

        def p2_ym(k):
            blk, s = k // 32, (k % 32) // 2
            S.waitt("vector", tokSc[k])
            S.waitt("vector", tok_bc[(blk, s)])
            S.wait("vector", "uP", k - 1)
            tokY[k] = S.emit("vector", lambda e, hl=k % 2, bs=s % 2, pl=k % 2:
                             e.tensor_tensor(out=p_b[pl][:], in0=hs_b[hl][:],
                                             in1=cbc_b[bs][:], op=OP.mult), inc="uYM")

        def p2_mm(k):
            blk, s, gi = k // 32, (k % 32) // 2, k % 2
            S.waitt("tensor", tokY[k])
            if s == 0:
                S.wait("tensor", "uYF", 2 * blk)
            for q in range(4):
                S.emit("tensor", lambda e, gi=gi, pl=k % 2, q=q, s=s: e.matmul(
                    psY[gi][:, q * 512:(q + 1) * 512], id_f32r[:],
                    p_b[pl][:, q * 512:(q + 1) * 512],
                    start=(s == 0), stop=(s == DS - 1)),
                    inc="uP" if q == 3 else None)

        def p2_finalize(blk):
            par = blk % 2
            for gi in range(2):
                g = 2 * blk + gi
                S.wait("vector", "uP", 32 * blk + 31 + gi)
                t1 = S.emit("vector", lambda e, g=g, gi=gi, par=par: e.scalar_tensor_tensor(
                    out=u_b[gi][:], in0=st_xc[par * 2 + gi][:], scalar=d_par[:, g:g + 1],
                    in1=psY[gi][:], op0=OP.mult, op1=OP.add), inc="cV")
                S.waitt("vector", tok_sz[blk])
                S.waitt("vector", tok_ybf[0])
                S.waitt("vector", t1)
                t2 = S.emit("vector", lambda e, gi=gi: e.tensor_tensor(
                    out=yb_b[:], in0=u_b[gi][:], in1=sz_b[gi][:], op=OP.mult), inc="uYF")
                S.waitt("sync", t2)
                tok_ybf[0] = S.dma("sync", s_y[:, g, :], yb_b[:], "dS")

        w_out_bf = A(768, 8192).bitcast(BF16).rearrange("p (g t) -> p g t", g=NG)
        w_r_bf = A(8960, 4096).bitcast(BF16).rearrange("p (g t) -> p g t", g=GC)
        w_k_bf = A(13056, 4096).bitcast(BF16).rearrange("p (g t) -> p g t", g=GC)
        w_v_bf = A(17152, 4096).bitcast(BF16).rearrange("p (g t) -> p g t", g=GC)
        ytile = A(21248, 4096).bitcast(BF16).rearrange("p (g t) -> p g t", g=NG)
        xn_win = A(25344, 4104).rearrange("p (g t) -> p g t", g=GC)
        x_tok3 = A(29448, 2048).rearrange("p (g t) -> p g t", g=2)
        tmp3 = A(31496, 4096)
        xr_bf = A(35592, 2048).bitcast(BF16).rearrange("p (g t) -> p g t", g=GC)
        xk_bf = A(37640, 2048).bitcast(BF16).rearrange("p (g t) -> p g t", g=GC)
        r_sb = A(39688, 4096)
        kact_bf = A(43784, 2048).bitcast(BF16).rearrange("p (g t) -> p g t", g=GC)
        out_tok = A(39688, 4096).rearrange("p (g t) -> p g t", g=4)

        tok_ybf = [None]
        for k0 in range(NSTEP):
            blk, s, gi = k0 // 32, (k0 % 32) // 2, k0 % 2
            if k0 % 32 == 0:
                if blk >= 1:
                    p2_finalize(blk - 1)
                if blk == 0:
                    p2_strips(0)
                p2_u(blk)
            if gi == 0:
                p2_bc(blk, s)
            p2_exp(k0)
            p2_bx(k0)
            p2_scan(k0)
            p2_ym(k0)
            p2_mm(k0)
            if k0 % 32 == 8 and blk + 1 < NB:
                p2_strips(blk + 1)
            if k0 % 32 == 24:
                p2_sz(blk)
        p2_finalize(NB - 1)

        S.wait("sync", "uAB", 32 * NB)
        S.wait("sync", "uYM", 32 * NB)

        S.waitt("sync", tok_ybf[0])
        S.waitt("sync", bn_dma[0])
        S.waitt("sync", bn_dma[1])
        S.dma("sync", ytile[:, :, :], s_y[:, :, 0:512], "dS", mark=False)
        S.dma("sync", xn_win[:, :, 1:513], s_xn[:, :, 0:512], "dS", mark=False)
        tyl0 = S.marker("sync", "dS")
        S.waitt("sync", tok_strip[7])
        S.waitt("sync", tok_pc_st[0])
        S.waitt("sync", tok_pc_st[1])
        tok_wout = S.dma("sync", w_out_bf[:, :, :], s_wq[:, :, :], "dS")

        stB = [A(29448, 1024), A(30472, 1024)]
        tokB3 = [None, None]

        def p3_castB(wbf, wd, kcs, gate_yf):
            last = None
            for kc in kcs:
                sl = kc % 2
                S.waitt("sync", tokB3[sl])
                tst = S.dma("sync", stB[sl], wd[:, kc, :], "dS")
                S.waitt("scalar", tst)
                if gate_yf:
                    S.wait("scalar", "uYF", 2 * NB)
                tokB3[sl] = last = S.emit("scalar", lambda e, wbf=wbf, kc=kc, sl=sl: e.copy(
                    out=wbf[:, kc, :], in_=stB[sl][:]), inc="cS")
            return last

        tok_wr = p3_castB(w_r_bf, w_r_d, range(GC), False)
        tok_wk = p3_castB(w_k_bf, w_k_d, range(GC), True)
        tok_wv = p3_castB(w_v_bf, w_v_d, range(GC), True)

        tmz = S.emit("vector", lambda e: e.memset(xn_win[:, :, 0:1], 0.0), inc="cV")
        S.waitt("tensor", tok_wout)
        S.wait("tensor", "uYF", 2 * NB)

        Tc3 = 512
        relu_buf = A(21248, 4096)
        out_tok3 = A(31496, 4096).rearrange("p (g t) -> p g t", g=4)
        psq = [psA[:, 0:1024], psA[:, 1024:2048], psB[:, 0:1024], psB[:, 1024:2048]]
        out_wv = [A(35592, 2048), A(37640, 2048)]
        tok_ow = [None, None]
        tok_tr_prev = None
        tok_chunk_c = None
        tok_f1a_prev = None
        tok_mix_done = None
        xslot_free = [None, None]
        tyl_next = txn_next = tyl0
        for ch in range(4):
            t0 = ch * Tc3
            tyl = txn = tyl_next

            S.waitt("tensor", tyl)
            tom_q = {}
            for half, pp in ((0, psA), (1, psB)):
                if ch > 0:
                    if half == 0:
                        S.waitt("tensor", tok_f1a_prev)
                    else:
                        S.waitt("tensor", tok_chunk_c)
                for mc in range(4):
                    m = half * 4 + mc
                    for kc in range(NG):
                        tom = S.emit("tensor", lambda e, m=m, mc=mc, kc=kc, pp=pp: e.matmul(
                            pp[:, mc * 512:(mc + 1) * 512],
                            w_out_bf[:, kc, m * 128:m * 128 + 128],
                            ytile[:, kc, :], start=(kc == 0), stop=(kc == NG - 1)),
                            inc="cT" if kc == NG - 1 else None)
                        if kc == NG - 1:
                            tom_q[(half, mc)] = tom
            S.waitt("scalar", tok_tr_prev)
            tev_q = {}
            for half, pp in ((0, psA), (1, psB)):
                for mc in range(4):
                    S.waitt("scalar", tom_q[(half, mc)])
                    tev_q[(half, mc)] = S.emit(
                        "scalar", lambda e, half=half, mc=mc, pp=pp: e.copy(
                            out=tmp3[:, half * 2048 + mc * 512:half * 2048 + mc * 512 + 512],
                            in_=pp[:, mc * 512:(mc + 1) * 512]), inc="cS")
            tev_a = tev_q[(0, 3)]
            tev_b = tev_q[(1, 3)]

            S.waitt("vector", txn)
            S.waitt("vector", tmz)
            S.waitt("vector", tok_chunk_c)
            S.waitt("vector", tok_ow[0])
            S.waitt("vector", tok_ow[1])
            tms = S.emit("vector", lambda e: e.tensor_tensor(
                out=r_sb[:].rearrange("p (g t) -> p g t", g=GC)[:, :, :],
                in0=xn_win[:, :, 1:513], in1=xn_win[:, :, 0:512], op=OP.subtract), inc="cV")
            S.waitt("vector", tms)
            tmr = None
            for gc in range(GC):
                tmr = S.emit("vector", lambda e, gc=gc: e.scalar_tensor_tensor(
                    out=xr_bf[:, gc, :], in0=r_sb[:, gc * 512:(gc + 1) * 512],
                    scalar=mu_r[:, gc:gc + 1], in1=xn_win[:, gc, 0:512],
                    op0=OP.mult, op1=OP.add), inc="cV" if gc == GC - 1 else None)
            tmixl = None
            for gc in range(GC):
                tmixl = S.emit("vector", lambda e, gc=gc: e.scalar_tensor_tensor(
                    out=xk_bf[:, gc, :], in0=r_sb[:, gc * 512:(gc + 1) * 512],
                    scalar=mu_k[:, gc:gc + 1], in1=xn_win[:, gc, 0:512],
                    op0=OP.mult, op1=OP.add), inc="cV" if gc == GC - 1 else None)
            tok_mix_done = tmixl

            sig_q = {}
            for half, pp in ((0, psA), (1, psB)):
                S.waitt("tensor", tok_wr)
                S.waitt("tensor", tmr)
                for mc in range(4):
                    m = half * 4 + mc
                    S.waitt("tensor", tev_q[(half, mc)])
                    trm = None
                    for kc in range(GC):
                        trm = S.emit("tensor", lambda e, m=m, mc=mc, kc=kc, pp=pp: e.matmul(
                            pp[:, mc * 512:(mc + 1) * 512],
                            w_r_bf[:, kc, m * 128:m * 128 + 128],
                            xr_bf[:, kc, :], start=(kc == 0), stop=(kc == GC - 1)),
                            inc="cT" if kc == GC - 1 else None)
                    S.waitt("scalar", trm)
                    S.waitt("scalar", tmixl)
                    sig_q[(half, mc)] = S.emit(
                        "scalar", lambda e, half=half, mc=mc, pp=pp: e.activation(
                            out=r_sb[:, half * 2048 + mc * 512:half * 2048 + mc * 512 + 512],
                            in_=pp[:, mc * 512:(mc + 1) * 512], func=AF.Sigmoid), inc="cS")
            tok_sig = [sig_q[(0, 3)], sig_q[(1, 3)]]
            kq = {}
            for half, pp in ((0, psA), (1, psB)):
                S.waitt("tensor", tok_wk)
                S.waitt("tensor", tmixl)
                for mc in range(4):
                    m = half * 4 + mc
                    S.waitt("tensor", sig_q[(half, mc)])
                    tkm = None
                    for kc in range(GC):
                        tkm = S.emit("tensor", lambda e, m=m, mc=mc, kc=kc, pp=pp: e.matmul(
                            pp[:, mc * 512:(mc + 1) * 512],
                            w_k_bf[:, kc, m * 128:m * 128 + 128],
                            xk_bf[:, kc, :], start=(kc == 0), stop=(kc == GC - 1)),
                            inc="cT" if kc == GC - 1 else None)
                    S.waitt("scalar", tkm)
                    trl = S.emit("scalar", lambda e, half=half, mc=mc, pp=pp: e.activation(
                        out=relu_buf[:, half * 2048 + mc * 512:half * 2048 + mc * 512 + 512],
                        in_=pp[:, mc * 512:(mc + 1) * 512], func=AF.Relu), inc="cS")
                    S.waitt("scalar", trl)
                    kq[(half, mc)] = S.emit(
                        "scalar", lambda e, half=half, mc=mc: e.activation(
                            out=kact_bf.rearrange("p g t -> p (g t)")[
                                :, half * 2048 + mc * 512:half * 2048 + mc * 512 + 512],
                            in_=relu_buf[:, half * 2048 + mc * 512:half * 2048 + mc * 512 + 512],
                            func=AF.Square), inc="cS")
            tok_k = [kq[(0, 3)], kq[(1, 3)]]
            if ch < 3:
                t0n = (ch + 1) * Tc3
                S.waitt("sync", tok_k[0])
                S.waitt("sync", tok_k[1])
                S.waitt("sync", tok_ybf[0])
                S.dma("sync", ytile[:, :, :], s_y[:, :, t0n:t0n + Tc3], "dS", mark=False)
                S.waitt("sync", tok_mix_done)
                S.dma("sync", xn_win[:, :, :], s_xn[:, :, t0n - 1:t0n + Tc3], "dS",
                      mark=False)
                tyl_next = txn_next = S.marker("sync", "dS")
            tvm_q = {}
            for half, pp in ((0, psA), (1, psB)):
                S.waitt("tensor", tok_wv)
                S.waitt("tensor", tok_k[0])
                S.waitt("tensor", tok_k[1])
                for mc in range(4):
                    m = half * 4 + mc
                    for kc in range(GC):
                        tvm = S.emit("tensor", lambda e, m=m, mc=mc, kc=kc, pp=pp: e.matmul(
                            pp[:, mc * 512:(mc + 1) * 512],
                            w_v_bf[:, kc, m * 128:m * 128 + 128],
                            kact_bf[:, kc, :], start=(kc == 0), stop=(kc == GC - 1)),
                            inc="cT" if kc == GC - 1 else None)
                        if kc == GC - 1:
                            tvm_q[(half, mc)] = tvm
            tf1a = tf1b = None
            for half, pp in ((0, psA), (1, psB)):
                for mc in range(4):
                    S.waitt("vector", tvm_q[(half, mc)])
                    S.waitt("vector", sig_q[(half, mc)])
                    tq = S.emit("vector", lambda e, half=half, mc=mc, pp=pp: e.tensor_tensor(
                        out=r_sb[:, half * 2048 + mc * 512:half * 2048 + mc * 512 + 512],
                        in0=r_sb[:, half * 2048 + mc * 512:half * 2048 + mc * 512 + 512],
                        in1=pp[:, mc * 512:(mc + 1) * 512], op=OP.mult), inc="cV")
                    if half == 0:
                        tf1a = tq
                    else:
                        tf1b = tq
            S.waitt("vector", tf1a)
            S.waitt("vector", tev_a)
            tf2a = S.emit("vector", lambda e: e.tensor_tensor(
                out=tmp3[:, 0:2048], in0=tmp3[:, 0:2048], in1=r_sb[:, 0:2048], op=OP.add),
                inc="cV")
            S.waitt("vector", tf1b)
            S.waitt("vector", tev_b)
            tf2b = S.emit("vector", lambda e: e.tensor_tensor(
                out=tmp3[:, 2048:4096], in0=tmp3[:, 2048:4096], in1=r_sb[:, 2048:4096],
                op=OP.add), inc="cV")
            t_add = []
            for wave in range(2):
                S.waitt("tensor", tf2a)
                S.waitt("tensor", tf2b)
                S.waitt("tensor", tf1b)
                if wave == 1:
                    S.waitt("tensor", t_add[1])
                ttr = None
                for ti in range(2):
                    tsub = wave * 2 + ti
                    for gc in range(GC):
                        ttr = S.emit("tensor", lambda e, tsub=tsub, ti=ti, gc=gc: e.transpose(
                            psB[:, ti * 1024 + gc * 128:ti * 1024 + (gc + 1) * 128],
                            tmp3.rearrange("p (g t) -> p g t", g=GC)[:, gc,
                                                                    tsub * 128:(tsub + 1) * 128],
                            id_f32[:]),
                            inc="cT" if gc == GC - 1 else None)
                    for _ in (0,):
                        pass
                for ti in range(2):
                    tsub = wave * 2 + ti
                    sl = tsub % 2
                    if ch == 0 and tsub < 2:
                        S.waitt("sync", tok_wv)
                    S.waitt("sync", xslot_free[sl])
                    txx = S.dma("sync", x_tok3[:, sl, :],
                                x_d.rearrange("(n p) c -> p n c", p=P)[:, ch * 4 + tsub, :],
                                "dS")
                    S.waitt("vector", ttr)
                    S.waitt("vector", txx)
                    t_add.append(S.emit("vector", lambda e, wave=wave, sl=sl, ti=ti:
                                        e.tensor_tensor(
                        out=out_wv[wave][:, ti * 1024:ti * 1024 + 1024],
                        in0=x_tok3[:, sl, :],
                        in1=psB[:, ti * 1024:ti * 1024 + 1024], op=OP.add), inc="cV"))
                    xslot_free[sl] = t_add[-1]
                S.waitt("sync", t_add[-1])
                tok_ow[wave] = S.dma(
                    "sync",
                    out_d.rearrange("(n p) c -> p n c", p=P)[:,
                                                             ch * 4 + wave * 2:
                                                             ch * 4 + wave * 2 + 2, :],
                    out_wv[wave][:].rearrange("p (n c) -> p n c", n=2), "dS")
            tok_chunk_c = t_add[-1]
            tok_f1a_prev = tf1a
            tok_tr_prev = ttr

        S.wait("sync", "dS", S.cnt["dS"])
        S.wait("sync", "dA", S.cnt["dA"])

        def replay(eng_obj, name):
            for it in S.items[name]:
                if it[0] == "w":
                    eng_obj.wait_ge(sems[it[1]], it[2])
                elif it[0] == "i":
                    eng_obj.sem_inc(sems[it[1]], 1)
                else:
                    _, fn, inc, amt = it
                    inst = fn(eng_obj)
                    if inc is not None:
                        inst.then_inc(sems[inc], amt)

        @block.sync
        def _(eng):
            replay(eng, "sync")

        @block.tensor
        def _(eng):
            replay(eng, "tensor")

        @block.vector
        def _(eng):
            replay(eng, "vector")

        @block.scalar
        def _(eng):
            replay(eng, "scalar")

        @block.gpsimd
        def _(eng):
            replay(eng, "gpsimd")

    return nc


def _relayout(w, ng):
    return np.ascontiguousarray(np.asarray(w, np.float32).reshape(ng, P, -1).transpose(1, 0, 2))


def kernel(L0_attn_x, L0_attn_W_in, L0_attn_conv_w, L0_attn_conv_b, L0_attn_W_xproj,
           L0_attn_W_dt, L0_attn_b_dt, L0_attn_A_log, L0_attn_D_param, L0_attn_W_out,
           L0_ffn_W_r, L0_ffn_W_k, L0_ffn_W_v, L0_ffn_mu_r, L0_ffn_mu_k):
    x = np.ascontiguousarray(np.asarray(L0_attn_x, dtype=np.float32))
    ident = np.eye(P, dtype=np.float32)
    shared = {
        "w_in": _relayout(L0_attn_W_in, GC),
        "w_xp": _relayout(L0_attn_W_xproj, NG),
        "w_dt": np.ascontiguousarray(np.asarray(L0_attn_W_dt, np.float32)),
        "w_out": _relayout(L0_attn_W_out, NG),
        "w_r": _relayout(np.ascontiguousarray(np.asarray(L0_ffn_W_r, np.float32).T), GC),
        "w_k": _relayout(np.ascontiguousarray(np.asarray(L0_ffn_W_k, np.float32).T), GC),
        "w_v": _relayout(np.ascontiguousarray(np.asarray(L0_ffn_W_v, np.float32).T), GC),
        "a_log": _relayout(L0_attn_A_log, NG),
        "conv_w": _relayout(L0_attn_conv_w, NG),
        "conv_b": np.ascontiguousarray(np.asarray(L0_attn_conv_b, np.float32).reshape(NG, P).T),
        "b_dt": np.ascontiguousarray(np.asarray(L0_attn_b_dt, np.float32).reshape(NG, P).T),
        "d_par": np.ascontiguousarray(np.asarray(L0_attn_D_param, np.float32).reshape(NG, P).T),
        "mu_r": np.ascontiguousarray(np.asarray(L0_ffn_mu_r, np.float32).reshape(GC, P).T),
        "mu_k": np.ascontiguousarray(np.asarray(L0_ffn_mu_k, np.float32).reshape(GC, P).T),
        "ident": ident,
        "identr": ident,
    }
    nc = build_nc()
    in_maps = [dict(shared, x=x[b]) for b in range(8)]
    res = run_bass_kernel_spmd(nc, in_maps, core_ids=list(range(8)))
    out = np.stack([np.asarray(res.results[b]["out"]) for b in range(8)], axis=0)
    return out.astype(np.float32)
